# version 45 (final)
# baseline (speedup 1.0000x reference)
import numpy as np
import ml_dtypes
import concourse.bass as bass
import concourse.mybir as mybir
from concourse.bass_utils import run_bass_kernel_spmd

F32 = mybir.dt.float32
BF16 = mybir.dt.bfloat16
AF = mybir.ActivationFunctionType
ALU = mybir.AluOpType
AX = mybir.AxisListType
bfdt = ml_dtypes.bfloat16

NCORES = 8
T = 2048
C = 1024
NTT = C // 128
D = 2048
KD = D // 128
NH = 16
QKV = 3 * D
MQKV = QKV // 128
FF = 5632
MFF = FF // 128
NG = C // 8
NB = NG // 4
EPS_LN = 1e-5
EPS_Q = 1e-8
QSCALE = float(1.0 / np.sqrt(128.0))
NW1 = FF * D

DEBUG_STAGE = None
ATTN_PARTS = 5
ENGINES = ("sync", "scalar", "vector", "tensor", "gpsimd")


class Builder:
    def __init__(self, nc):
        self.nc = nc
        self.streams = {k: [] for k in ENGINES}
        self.meta = {k: [] for k in ENGINES}
        self.pc = {}
        self.waited = {k: {} for k in ENGINES}
        self.sems = {}
        self.ready = {}
        self.cons = {}
        self.dbg_outs = {}

    def sem(self, name):
        if name not in self.sems:
            self.sems[name] = self.nc.semaphore(name).__enter__()
            self.pc[name] = 0
        return self.sems[name]

    def emit(self, eng, fn, inc=None, inc_k=1):
        if inc is not None:
            self.sem(inc)
            self.pc[inc] += inc_k
            val = self.pc[inc]
            sem = self.sems[inc]

            def wrapped(e, fn=fn, sem=sem, k=inc_k):
                fn(e).then_inc(sem, k)
            self.streams[eng].append(wrapped)
            self.meta[eng].append(("i", inc, inc_k))
            return (inc, val)
        self.streams[eng].append(fn)
        self.meta[eng].append(("o",))
        return None

    def wait(self, eng, sem_name, val):
        if val is None or val <= 0:
            return
        w = self.waited[eng]
        if w.get(sem_name, 0) >= val:
            return
        w[sem_name] = val
        sem = self.sem(sem_name)
        self.streams[eng].append(lambda e, s=sem, v=val: e.wait_ge(s, v))
        self.meta[eng].append(("w", sem_name, val))

    def produced(self, buf, mark):
        self.ready[buf] = mark
        self.cons[buf] = []

    def consumed(self, buf, mark):
        self.cons.setdefault(buf, []).append(mark)

    def wait_ready(self, eng, buf):
        m = self.ready.get(buf)
        if m:
            self.wait(eng, m[0], m[1])

    def wait_free(self, eng, buf):
        for m in self.cons.get(buf, ()):
            self.wait(eng, m[0], m[1])
        m = self.ready.get(buf)
        if m:
            self.wait(eng, m[0], m[1])

    def join(self, eng, marks, tag="sJ"):
        for sm, v in marks:
            self.wait(eng, sm, v)
        self.sem(tag)
        self.pc[tag] += 1
        val = self.pc[tag]
        sem = self.sems[tag]
        self.streams[eng].append(lambda e, s=sem: e.sem_inc(s, 1))
        self.meta[eng].append(("i", tag, 1))
        return (tag, val)

    def check_deadlock(self):
        pos = {k: 0 for k in ENGINES}
        cnt = {}
        progress = True
        while progress:
            progress = False
            for eng in ENGINES:
                ms = self.meta[eng]
                while pos[eng] < len(ms):
                    m = ms[pos[eng]]
                    if m[0] == "w" and cnt.get(m[1], 0) < m[2]:
                        break
                    if m[0] == "i":
                        cnt[m[1]] = cnt.get(m[1], 0) + m[2]
                    pos[eng] += 1
                    progress = True
        stuck = {e: (pos[e], len(self.meta[e])) for e in ENGINES if pos[e] < len(self.meta[e])}
        if stuck:
            detail = {e: self.meta[e][pos[e]] for e in stuck}
            raise RuntimeError(f"deadlock in static schedule: {stuck} blocked at {detail}")


def build_program(dbg_stage=None):
    nc = bass.Bass("TRN2", target_bir_lowering=False)
    b = Builder(nc)
    b.dbg = dbg_stage

    x_in = nc.dram_tensor("x", [T, D], F32, kind="ExternalInput")
    wqkv = nc.dram_tensor("wqkv", [D, QKV], F32, kind="ExternalInput")
    wout = nc.dram_tensor("wout", [D, D], F32, kind="ExternalInput")
    bout = nc.dram_tensor("bout", [128, KD], F32, kind="ExternalInput")
    w1 = nc.dram_tensor("w1", [D, FF], F32, kind="ExternalInput")
    w2 = nc.dram_tensor("w2", [D, FF], F32, kind="ExternalInput")
    w3 = nc.dram_tensor("w3", [FF, D], F32, kind="ExternalInput")
    ident_in = nc.dram_tensor("ident", [128, 128], BF16, kind="ExternalInput")
    mask_in = nc.dram_tensor("mask01", [128, 512], BF16, kind="ExternalInput")
    y_out = nc.dram_tensor("y", [T, D], F32, kind="ExternalOutput")

    actscr = nc.dram_tensor("actscr", [C, D], BF16, kind="Internal")
    act2scr = nc.dram_tensor("act2scr", [T, D], BF16, kind="Internal")
    yscr = nc.dram_tensor("yscr", [D, C], BF16, kind="Internal")
    xbscr = nc.dram_tensor("xbscr", [T, D], BF16, kind="Internal")
    wqkvb = nc.dram_tensor("wqkvb", [MQKV, 128, D], BF16, kind="Internal")
    woutb = nc.dram_tensor("woutb", [KD, 128, D], BF16, kind="Internal")
    qw1 = nc.dram_tensor("qw1", [FF, D], BF16, kind="Internal")
    qw2 = nc.dram_tensor("qw2", [FF, D], BF16, kind="Internal")
    qw3 = nc.dram_tensor("qw3", [D, FF], BF16, kind="Internal")
    qw1T = nc.dram_tensor("qw1T", [D, FF], BF16, kind="Internal")
    qw2T = nc.dram_tensor("qw2T", [D, FF], BF16, kind="Internal")
    qw3T = nc.dram_tensor("qw3T", [FF, D], BF16, kind="Internal")

    BA = 16512
    AT = lambda name, shape, dt, off: nc.alloc_sbuf_tensor_at(name, shape, dt, offset=BA + off)
    xb = AT("xb", [128, NTT * D], BF16, 0)
    ident = AT("identS", [128, 128], BF16, 32768)
    mask01 = AT("mask01S", [128, 512], BF16, 33024)
    bias16 = AT("bias16", [128, KD], F32, 34048)
    onesf = AT("onesf", [128, 128], F32, 34112)
    stats = AT("stats", [128, 24], F32, 34624)
    mv = AT("mv", [128, 16], F32, 34720)
    rstd8 = AT("rstd8", [128, 8], F32, 34784)
    nbias8 = AT("nbias8", [128, 8], F32, 34816)
    lntmp = AT("lntmp", [128, 8], F32, 34848)
    hgam = AT("hgam", [128, 8], F32, 34880)
    sums4 = AT("sums4", [128, 16], F32, 34912)
    gpart = AT("gpart", [128, 384], F32, 34976)
    epsc = AT("epsc", [128, 4], F32, 36512)
    sexp = AT("sexp", [128, 1024], BF16, 36544)
    wdiv = AT("wdiv", [128, 1024], BF16, 38592)
    wtb = AT("wtb", [128, 1024], BF16, 40640)
    vtb = AT("vtb", [128, 1024], BF16, 42688)
    sigst = AT("sigst", [128, 4096], BF16, 36544)
    ttok = AT("ttok", [128, 2 * D], BF16, 44736)
    ttf0 = AT("ttf0", [128, 1024], F32, 44736)
    ttf1 = AT("ttf1", [128, 1024], F32, 48832)
    qg = AT("qg", [128, NG * 128], BF16, 52928)
    kg = AT("kg", [128, NG * 128], BF16, 85696)
    vg = AT("vg", [128, NG * 128], BF16, 118464)
    hid = AT("hid", [128, MFF * C], BF16, 52928)
    ytokB = AT("ytokB", [128, 2 * D], BF16, 118464)
    ofinC0 = AT("ofinC0", [128, 2 * D], F32, 126656)
    bandw1a = AT("bandw1a", [128, KD * 256], BF16, 184000)
    bandw1b = AT("bandw1b", [128, KD * 256], BF16, 192192)
    bandw2a = AT("bandw2a", [128, KD * 256], BF16, 143040)
    bandw2b = AT("bandw2b", [128, KD * 256], BF16, 200384)
    band3 = AT("band3", [128, 2 * MFF * 128], BF16, 184000)
    t1f = AT("t1f", [128, KD * C], BF16, 151232)
    wsf = AT("wsf", [128, 2048], F32, 184000)
    wsfq0 = AT("wsfq0", [128, 1024], BF16, 184000)
    wsfq1 = AT("wsfq1", [128, 1024], BF16, 188096)
    wsb = AT("wsb", [128, 2 * D], BF16, 192192)
    xstg = AT("xstg", [128, 2048], F32, 200384)
    xstq0 = AT("xstq0", [128, 1024], BF16, 200384)
    xstq1 = AT("xstq1", [128, 1024], BF16, 204480)
    ytok = AT("ytok", [128, 2 * D], BF16, 200384)
    xcst = AT("xcst", [128, 2048], BF16, 208576)
    xcF = AT("xcF", [128, 1024], F32, 208576)
    ofin = AT("ofin", [128, 2 * D], F32, 184000)

    from contextlib import ExitStack
    with ExitStack() as st:
        e = st.enter_context
        pbig = [e(nc.psum_tensor(f"pbig{i}", [128, 512], F32)) for i in range(3)]
        pscore = [e(nc.psum_tensor(f"pscore{i}", [128, 512], F32)) for i in range(2)]
        pmix = e(nc.psum_tensor("pmix", [128, 512], F32))
        pta = e(nc.psum_tensor("pta", [128, 1024], BF16))
        ptb = e(nc.psum_tensor("ptb", [128, 1024], BF16))
        ptx = [pta, ptb]
        pbig4 = pbig + [pscore[0]]
        pmisc = pscore[0]
        blocks = e(nc.Block())

        ctx = dict(locals())
        _build_all(b, ctx)
        b.check_deadlock()

        blocks.sync(lambda eng: [fn(eng) for fn in b.streams["sync"]])
        blocks.scalar(lambda eng: [fn(eng) for fn in b.streams["scalar"]])
        blocks.vector(lambda eng: [fn(eng) for fn in b.streams["vector"]])
        blocks.tensor(lambda eng: [fn(eng) for fn in b.streams["tensor"]])
        blocks.gpsimd(lambda eng: [fn(eng) for fn in b.streams["gpsimd"]])

    return nc, b


def _build_all(b, ctx):
    nc = b.nc
    g = lambda n: ctx[n]

    b.emit("sync", lambda e, a=g("ident"), s=g("ident_in"): e.dma_start(out=a[:], in_=s[:, :]),
           inc="dC", inc_k=16)
    b.emit("sync", lambda e, a=g("mask01"), s=g("mask_in"): e.dma_start(out=a[:], in_=s[:, :]),
           inc="dC", inc_k=16)
    b.emit("sync", lambda e, a=g("bias16"), s=g("bout"): e.dma_start(out=a[:], in_=s[:, :]),
           inc="dC", inc_k=16)
    b.produced("consts", ("dC", b.pc["dC"]))
    b.emit("vector", lambda e, a=g("onesf"): nc.vector.memset(a[:], 1.0), inc="sV")
    b.produced("ones", ("sV", b.pc["sV"]))
    b.emit("vector", lambda e, a=g("lntmp"): nc.vector.memset(a[:], 0.0), inc="sV")
    b.produced(("lntmp", 3), ("sV", b.pc["sV"]))
    b.emit("vector", lambda e, a=g("epsc"): nc.vector.memset(a[:], EPS_LN), inc="sV")
    b.produced("epsc", ("sV", b.pc["sV"]))

    _gamma_init(b, ctx)
    _quant_init(b, ctx)

    _phase_xload(b, ctx, chunk=0, precast=False)
    _phase_ln_t(b, ctx, chunk=0)
    if b.dbg == "t1":
        _dump(b, ctx, "t1f", g("t1f")); return
    _phase_xload(b, ctx, chunk=1, precast=True)
    _phase_qkv(b, ctx, chunk=0)
    _gamma_final_one(b, ctx, 0)
    if b.dbg == "qkv":
        for n in ("qg", "kg", "vg"):
            _dump(b, ctx, n, g(n))
        return
    _phase_attn_inner(b, ctx, chunk=0)
    _gamma_final_one(b, ctx, 1)
    if b.dbg == "attn":
        _dump(b, ctx, "atf", g("t1f")); return
    _phase_wout(b, ctx, chunk=0)
    _gamma_final_one(b, ctx, 2)
    _phase_y_ln_resid(b, ctx, chunk=0)
    if b.dbg == "xp":
        _dump(b, ctx, "xb", g("xb")); return

    _phase_xread_bf16(b, ctx, chunk=1)
    _phase_ln_t(b, ctx, chunk=1)
    _phase_qkv(b, ctx, chunk=1)
    _phase_attn_inner(b, ctx, chunk=1)
    _phase_wout(b, ctx, chunk=1)
    _ffn_t2f_load(b, ctx, 0)
    ctx["_t2f_loaded_0"] = True
    _quant_step(b, ctx, 10 ** 9)
    _phase_y_ln_resid(b, ctx, chunk=1)
    if b.dbg == "xp2":
        _dump(b, ctx, "xb", g("xb")); return
    if b.dbg == "quant":
        _dump_dram(b, ctx, "qw1", g("qw1")); return

    if b.dbg == "qwt":
        for n in ("qw1T", "qw2T", "qw3T"):
            _dump_dram(b, ctx, n, g(n))
        return
    for chunk in range(2):
        _phase_ffn(b, ctx, chunk)
        if b.dbg == "of":
            _dump_dram(b, ctx, "yscr", g("yscr")); return
        if b.dbg == "hid":
            _dump(b, ctx, "hid", g("hid")); return
        if b.dbg == "t2f":
            _dump(b, ctx, "t2f", g("t1f")); return
    b.wait("sync", "dO", b.pc["dO"])
    b.wait("gpsimd", "dO", b.pc["dO"])


def _dump(b, ctx, name, sb):
    nc = b.nc
    o = nc.dram_tensor("dbg_" + name, [sb.shape[0], sb.shape[1]], sb.dtype,
                       kind="ExternalOutput")
    for s in sorted(b.pc):
        if s != "dO":
            b.wait("gpsimd", s, b.pc[s])
    b.emit("gpsimd", lambda e, o=o, sb=sb: e.dma_start(out=o[:, :], in_=sb[:]),
           inc="dO", inc_k=16)
    b.wait("gpsimd", "dO", b.pc["dO"])
    b.dbg_outs[name] = o


def _dump_dram(b, ctx, name, dr):
    nc = b.nc
    o = nc.dram_tensor("dbg_" + name, [dr.shape[0], dr.shape[1]], dr.dtype,
                       kind="ExternalOutput")
    for s in sorted(b.pc):
        if s != "dO":
            b.wait("gpsimd", s, b.pc[s])
    b.emit("gpsimd", lambda e, o=o, dr=dr: e.dma_start(out=o[:, :], in_=dr[:, :]),
           inc="dO", inc_k=16)
    b.wait("gpsimd", "dO", b.pc["dO"])
    b.dbg_outs[name] = o


def _phase_xload(b, ctx, chunk, precast):
    nc = b.nc
    x_in, xb, xbscr = (ctx[k] for k in ("x_in", "xb", "xbscr"))
    if not precast:
        for i in range(NTT):
            b.wait_free("gpsimd", ("xb", i))
            b.emit("gpsimd", lambda e, d=xb[:, i * D:(i + 1) * D],
                   s=x_in[chunk * C + i * 128:chunk * C + (i + 1) * 128, :]:
                   e.dma_start(out=d, in_=s), inc=f"dXc{i}", inc_k=16)
            m = (f"dXc{i}", b.pc[f"dXc{i}"])
            b.produced(("xb", i), m)
            b.produced(("xdata", chunk, i), m)
    else:
        b.emit("gpsimd", lambda e, d=xbscr[chunk * C:(chunk + 1) * C, :],
               s=x_in[chunk * C:(chunk + 1) * C, :]:
               e.dma_start(out=d, in_=s), inc="dPr", inc_k=16)
        b.produced("x1pre", ("dPr", b.pc["dPr"]))


def _phase_xread_bf16(b, ctx, chunk):
    xb, xbscr = ctx["xb"], ctx["xbscr"]
    b.wait_ready("sync", "x1pre")
    for i in range(NTT):
        b.wait_free("sync", ("xb", i))
        b.emit("sync", lambda e, d=xb[:, i * D:(i + 1) * D],
               s=xbscr[chunk * C + i * 128:chunk * C + (i + 1) * 128, :]:
               e.dma_start(out=d, in_=s), inc=f"dXr{i}", inc_k=16)
        m = (f"dXr{i}", b.pc[f"dXr{i}"])
        b.produced(("xb", i), m)
        b.produced(("xdata", chunk, i), m)


def _ln_stats_apply(b, ctx, src_tile_ap, src_key, i, out_ap, out_key):
    nc = b.nc
    stats, mv, rstd8, nbias8, lntmp = (ctx[k] for k in ("stats", "mv", "rstd8", "nbias8", "lntmp"))
    b.wait_free("vector", ("stats",))
    if src_key is not None:
        b.wait_ready("vector", src_key)
    mm = None
    for j in range(4):
        mm = b.emit("vector", lambda e, d=stats[:, j * 6:(j + 1) * 6],
                    s=src_tile_ap[:, j * 512:(j + 1) * 512]: nc.vector.bn_stats(d, s),
                    inc="sV" if j == 3 else None)
    b.produced(("stats",), mm)
    b.wait_ready("vector", ("stats",))
    b.wait_free("vector", ("mv", i))
    m_ag = b.emit("vector", lambda e, d=mv[:, 2 * i:2 * i + 2], s=stats[:]:
                  nc.vector.bn_aggr(d, s), inc="sV")
    b.consumed(("stats",), m_ag)
    b.produced(("mv", i), m_ag)
    b.wait_ready("scalar", ("mv", i))
    b.wait_ready("scalar", "epsc")
    b.wait_free("scalar", ("lntmp", i))
    epsc = ctx["epsc"]
    m_ln = b.emit("scalar", lambda e, d=lntmp[:, i:i + 1], s=mv[:, 2 * i + 1:2 * i + 2],
                  bb=epsc[:, 0:1]:
                  nc.scalar.activation(d, s, AF.Ln, bias=bb), inc="sS")
    b.produced(("lntmp", i), m_ln)
    b.wait("scalar", "sS", m_ln[1])
    b.wait_free("scalar", ("rstd", i))
    m_rs = b.emit("scalar", lambda e, d=rstd8[:, i:i + 1], s=lntmp[:, i:i + 1]:
                  nc.scalar.activation(d, s, AF.Exp, scale=-0.5), inc="sS")
    b.consumed(("lntmp", i), m_rs)
    b.produced(("rstd", i), m_rs)
    b.wait_ready("vector", ("rstd", i))
    b.wait_free("vector", ("nbias", i))
    m_nb = b.emit("vector", lambda e, d=nbias8[:, i:i + 1], s=mv[:, 2 * i:2 * i + 1],
                  r=rstd8[:, i:i + 1]: nc.vector.tensor_scalar(d, s, r, -1.0, ALU.mult, ALU.mult),
                  inc="sV")
    b.consumed(("mv", i), m_nb)
    b.produced(("nbias", i), m_nb)
    b.wait_ready("vector", ("nbias", i))
    m_ap = b.emit("vector", lambda e, d=out_ap, s=src_tile_ap,
                  r=rstd8[:, i:i + 1], nb=nbias8[:, i:i + 1]:
                  nc.vector.tensor_scalar(d, s, r, nb, ALU.mult, ALU.add), inc="sV")
    b.consumed(("rstd", i), m_ap)
    b.consumed(("nbias", i), m_ap)
    if src_key is not None:
        b.consumed(src_key, m_ap)
    b.produced(out_key, m_ap)
    return m_ap


def _phase_ln_t(b, ctx, chunk):
    nc = b.nc
    xb, ttok, t1f, ident = (ctx[k] for k in ("xb", "ttok", "t1f", "ident"))
    ptx = ctx["ptx"]
    if chunk == 0:
        b.wait_ready("tensor", "consts")
    last_ev = None
    for i in range(NTT):
        sl = i % 2
        b.wait_free("vector", ("ttok", sl))
        _ln_stats_apply(b, ctx, xb[:, i * D:(i + 1) * D], ("xb", i), i,
                        ttok[:, sl * D:(sl + 1) * D], ("ttok", sl))
        if i == 0:
            for f in range(KD):
                b.wait_free("tensor", ("t1f", f))
        mt_last = None
        for k8 in range(2):
            pkey = ("ptbank", k8)
            pbank = ptx[k8]
            b.wait_free("tensor", pkey)
            b.wait_ready("tensor", ("ttok", sl))
            for kk in range(8):
                k = k8 * 8 + kk
                mt_last = b.emit("tensor", lambda e, p=pbank[:, kk * 128:(kk + 1) * 128],
                                 s=ttok[:, sl * D + k * 128: sl * D + (k + 1) * 128],
                                 idn=ident[:]:
                                 nc.tensor.transpose(p, s, idn),
                                 inc="sT" if kk == 7 else None)
            b.produced(pkey, mt_last)
            b.wait_ready("vector", pkey)
            dst = t1f[:, :].rearrange("p (k c) -> p k c", c=C)[
                :, k8 * 8:(k8 + 1) * 8, i * 128:(i + 1) * 128]
            m_e = b.emit("vector", lambda e, d=dst,
                         s=pbank[:].rearrange("p (k f) -> p k f", k=8):
                         nc.vector.tensor_copy(d, s), inc="sV")
            b.consumed(pkey, m_e)
            last_ev = m_e
        b.consumed(("ttok", sl), mt_last)
        _gamma_step(b, ctx, 1)
    for f in range(KD):
        b.produced(("t1f", f), last_ev)
    b.produced("t1f", last_ev)


def _stream_wslab(b, ctx, chunk, wsrc, wbscr, m):
    nc = b.nc
    wsf, wsb = ctx["wsf"], ctx["wsb"]
    sl = m % 2
    wslab = ("wsb", sl)
    if chunk == 0:
        half_marks = []
        for q4 in range(4):
            fslq = ("wsf", q4 // 2)
            if q4 % 2 == 0:
                b.wait_free("sync", fslq)
            dst = wsf[:, q4 * 512:(q4 + 1) * 512].rearrange("p (k c) -> p k c", k=4)
            src = wsrc[q4 * 512:(q4 + 1) * 512, m * 128:(m + 1) * 128].rearrange(
                "(k p) c -> p k c", p=128)
            qm = b.emit("sync", lambda e, d=dst, s=src: e.dma_start(out=d, in_=s),
                        inc=f"dWf{q4 // 2}", inc_k=16)
            if q4 % 2 == 1:
                b.produced(fslq, qm)
        b.wait_free("vector", wslab)
        m_c = None
        for hh in range(2):
            b.wait_ready("vector", ("wsf", hh))
            m_c = b.emit("vector", lambda e, d=wsb[:, sl * D + hh * 1024: sl * D + (hh + 1) * 1024],
                         s=wsf[:, hh * 1024:(hh + 1) * 1024]:
                         nc.vector.tensor_copy(d, s), inc="sV")
            b.consumed(("wsf", hh), m_c)
        b.produced(wslab, m_c)
        b.wait_ready("gpsimd", wslab)
        pref = "dWq" if wbscr.name.startswith("wqkvb") else "dWo"
        m_st = b.emit("gpsimd", lambda e, d=wbscr[m, :, :], s=wsb[:, sl * D:(sl + 1) * D]:
                      e.dma_start(out=d, in_=s), inc=f"{pref}{sl}", inc_k=16)
        b.consumed(wslab, m_st)
        if m == (MQKV - 1 if wbscr.name.startswith("wqkvb") else KD - 1):
            b.produced("wst_done_" + wbscr.name[:4] + "_0", (f"{pref}0", b.pc[f"{pref}0"]))
            b.produced("wst_done_" + wbscr.name[:4] + "_1", (f"{pref}1", b.pc[f"{pref}1"]))
    else:
        b.wait_free("sync", wslab)
        b.wait_ready("sync", "wst_done_" + wbscr.name[:4] + "_0")
        b.wait_ready("sync", "wst_done_" + wbscr.name[:4] + "_1")
        m_l = b.emit("sync", lambda e, d=wsb[:, sl * D:(sl + 1) * D], s=wbscr[m, :, :]:
                     e.dma_start(out=d, in_=s), inc=f"dWb{sl}", inc_k=16)
        b.produced(wslab, m_l)
    return wslab, sl


def _phase_qkv(b, ctx, chunk):
    nc = b.nc
    wqkv, wqkvb, wsb, t1f, pbig = (ctx[k] for k in ("wqkv", "wqkvb", "wsb", "t1f", "pbig"))
    dst_of = {0: ctx["qg"], 1: ctx["kg"], 2: ctx["vg"]}
    nm_of = {0: "qg", 1: "kg", 2: "vg"}
    for w in ("qg", "kg", "vg"):
        b.wait_free("scalar", w)
    last_mm = None
    for m in range(MQKV):
        wslab, sl = _stream_wslab(b, ctx, chunk, wqkv, wqkvb, m)
        which, head = divmod(m, NH)
        for sub in range(2):
            pb = (m * 2 + sub) % 3
            pbuf = ("pbig", pb)
            b.wait_free("tensor", pbuf)
            b.wait_ready("tensor", wslab)
            b.wait_ready("tensor", "t1f")
            mm = None
            for k in range(KD):
                mm = b.emit("tensor", lambda e, p=pbig[pb][:],
                            w=wsb[:, sl * D + k * 128: sl * D + (k + 1) * 128],
                            r=t1f[:, k * C + sub * 512: k * C + sub * 512 + 512],
                            st=(k == 0), sp=(k == KD - 1):
                            nc.tensor.matmul(p, w, r, start=st, stop=sp),
                            inc="sT" if k == KD - 1 else None)
            b.produced(pbuf, mm)
            last_mm = mm
            if sub == 1:
                b.consumed(wslab, mm)
            b.wait_ready("scalar", pbuf)
            dst_ap = dst_of[which][:].rearrange("p (gg t h) -> p gg t h", t=8, h=16)[
                :, 64 * sub:64 * (sub + 1), :, head:head + 1]
            scale = QSCALE if which == 0 else 1.0
            m_e = b.emit("scalar", lambda e, d=dst_ap, s=pbig[pb][:], sc=scale:
                         nc.scalar.activation(d, s, AF.Copy, scale=sc), inc="sS")
            b.consumed(pbuf, m_e)
        if chunk == 0:
            _gamma_step(b, ctx, 2)
        else:
            _quant_step(b, ctx, 2)
    for f in range(KD):
        b.consumed(("t1f", f), last_mm)
    mark = ("sS", b.pc["sS"])
    for n in ("qkvg", "qg", "kg", "vg"):
        b.produced(n, mark)


def _attn_part1(b, ctx, chunk, bt):
    nc = b.nc
    (qg, kg, pscore, sexp, sums4, wdiv, mask01) = (ctx[k] for k in (
        "qg", "kg", "pscore", "sexp", "sums4", "wdiv", "mask01"))
    if bt == 0:
        b.wait_ready("tensor", "qkvg")
        b.wait_ready("tensor", "consts")
    pb = bt % 2
    h0 = pb * 512
    ps, sx, sm, wv = ("pscore", pb), ("sexp", pb), ("sums4", pb), ("wdiv", pb)
    b.wait_free("tensor", ps)
    mm = None
    for g4 in range(4):
        gi = bt * 4 + g4
        mm = b.emit("tensor", lambda e, p=pscore[pb][:, g4 * 128:(g4 + 1) * 128],
                    q=qg[:, gi * 128:(gi + 1) * 128], k=kg[:, gi * 128:(gi + 1) * 128]:
                    nc.tensor.matmul(p, q, k, start=True, stop=True),
                    inc="sT" if g4 == 3 else None)
    b.produced(ps, mm)
    ctx["_attn_lastsc"] = mm
    b.wait_free("scalar", sx)
    b.wait_ready("scalar", ps)
    m_x = b.emit("scalar", lambda e, d=sexp[:, h0:h0 + 512], s=pscore[pb][:]:
                 nc.scalar.activation(d, s, AF.Exp), inc="sS")
    b.consumed(ps, m_x)
    b.produced(sx, m_x)
    b.wait_free("vector", wv)
    b.wait_ready("vector", sx)
    b.wait_ready("vector", "consts")
    m_mm = b.emit("vector", lambda e, d=wdiv[:, h0:h0 + 512], s=sexp[:, h0:h0 + 512],
                  mk=mask01[:]:
                  nc.vector.tensor_mul(d, s, mk), inc="sV")
    b.consumed(sx, m_mm)
    b.produced(wv, m_mm)
    b.wait_free("vector", sm)
    b.wait("vector", "sV", m_mm[1])
    m_s = b.emit("vector", lambda e, d=sums4[:, pb * 8:pb * 8 + 4],
                 s=wdiv[:, h0:h0 + 512].rearrange("p (g x) -> p g x", g=4):
                 nc.vector.tensor_reduce(d, s, AX.X, ALU.add), inc="sV")
    b.wait("vector", "sV", m_s[1])
    m_r = b.emit("vector", lambda e, d=sums4[:, pb * 8 + 4:pb * 8 + 8],
                 s=sums4[:, pb * 8:pb * 8 + 4]:
                 nc.vector.reciprocal(d, s), inc="sV")
    b.produced(sm, m_r)
    b.wait_free("vector", sx)
    m_d = None
    for g4 in range(4):
        b.wait("vector", "sV", m_r[1])
        m_d = b.emit("vector", lambda e, d=sexp[:, h0 + g4 * 128:h0 + (g4 + 1) * 128],
                     s=wdiv[:, h0 + g4 * 128:h0 + (g4 + 1) * 128],
                     r=sums4[:, pb * 8 + 4 + g4:pb * 8 + 5 + g4]:
                     nc.vector.tensor_scalar(d, s, r, None, ALU.mult), inc="sV")
    b.produced(sx, m_d)
    b.consumed(wv, m_d)
    b.consumed(sm, m_d)


def _attn_part2(b, ctx, chunk, bt):
    nc = b.nc
    (vg, pmix, sexp, wtb, vtb, ident, t1f) = (ctx[k] for k in (
        "vg", "pmix", "sexp", "wtb", "vtb", "ident", "t1f"))
    atf = t1f
    ptx = ctx["ptx"]
    pb = bt % 2
    h0 = pb * 512
    sx, wt, vt = ("sexp", pb), ("wtb",), ("vtb",)
    pv = ("ptbank", 0)
    b.wait_free("tensor", vt)
    b.wait_free("tensor", pv)
    mtv = None
    for g4 in range(4):
        gi = bt * 4 + g4
        mtv = b.emit("tensor", lambda e, p=ptx[0][:, g4 * 128:(g4 + 1) * 128],
                     s=vg[:, gi * 128:(gi + 1) * 128], idn=ident[:]:
                     nc.tensor.transpose(p, s, idn), inc="sT" if g4 == 3 else None)
    b.produced(pv, mtv)
    ctx["_attn_lasttv"] = mtv
    b.wait_ready("vector", pv)
    mev = b.emit("vector", lambda e, d=vtb[:, 0:512], s=ptx[0][:, 0:512]:
                 nc.vector.tensor_copy(d, s), inc="sV")
    b.consumed(pv, mev)
    b.produced(vt, mev)
    pw = ("ptbank", 1)
    b.wait_free("tensor", wt)
    b.wait_free("tensor", pw)
    b.wait_ready("tensor", sx)
    mtw = None
    for g4 in range(4):
        mtw = b.emit("tensor", lambda e, p=ptx[1][:, g4 * 128:(g4 + 1) * 128],
                     s=sexp[:, h0 + g4 * 128:h0 + (g4 + 1) * 128], idn=ident[:]:
                     nc.tensor.transpose(p, s, idn), inc="sT" if g4 == 3 else None)
    b.produced(pw, mtw)
    b.consumed(sx, mtw)
    b.wait_ready("vector", pw)
    mew = b.emit("vector", lambda e, d=wtb[:, 0:512], s=ptx[1][:, 0:512]:
                 nc.vector.tensor_copy(d, s), inc="sV")
    b.consumed(pw, mew)
    b.produced(wt, mew)
    b.wait_free("tensor", ("pmix",))
    b.wait_ready("tensor", wt)
    b.wait_ready("tensor", vt)
    mmx = None
    for g4 in range(4):
        mmx = b.emit("tensor", lambda e, p=pmix[:, g4 * 128:(g4 + 1) * 128],
                     v=vtb[:, g4 * 128:(g4 + 1) * 128],
                     w=wtb[:, g4 * 128:(g4 + 1) * 128]:
                     nc.tensor.matmul(p, v, w, start=True, stop=True),
                     inc="sT" if g4 == 3 else None)
    b.produced(("pmix",), mmx)
    b.consumed(wt, mmx)
    b.consumed(vt, mmx)
    b.wait_ready("scalar", ("pmix",))
    dst = atf[:].rearrange("p (h gg t) -> p gg t h", h=16, t=8)[:, bt * 4:(bt + 1) * 4, :, :]
    src = pmix[:].rearrange("p (g t h) -> p g t h", t=8, h=16)
    m_de = b.emit("scalar", lambda e, d=dst, s=src:
                  nc.scalar.activation(d, s, AF.Copy), inc="sS")
    b.consumed(("pmix",), m_de)
    if chunk == 0:
        _gamma_step(b, ctx, 3)
    else:
        _quant_step(b, ctx, 3)
    if bt == NB - 1:
        b.consumed("qg", ctx["_attn_lastsc"])
        b.consumed("kg", ctx["_attn_lastsc"])
        b.consumed("vg", ctx["_attn_lasttv"])
        b.produced("atf", m_de)
        for f in range(KD):
            b.produced(("t1f", f), m_de)


def _phase_attn_inner(b, ctx, chunk):
    _attn_part1(b, ctx, chunk, 0)
    for bt in range(1, NB):
        _attn_part1(b, ctx, chunk, bt)
        _attn_part2(b, ctx, chunk, bt - 1)
    _attn_part2(b, ctx, chunk, NB - 1)


def _phase_wout(b, ctx, chunk):
    nc = b.nc
    wout, woutb, wsb, pbig, bias16, yscr, t1f = (
        ctx[k] for k in ("wout", "woutb", "wsb", "pbig", "bias16", "yscr", "t1f"))
    atf = t1f
    ytmp = ctx["qg"]
    b.wait_free("vector", "qg")
    wr = []
    last_mm = None
    for m in range(KD):
        wslab, sl = _stream_wslab(b, ctx, chunk, wout, woutb, m)
        evs = []
        for sub in range(2):
            pb = (m * 2 + sub) % 3
            pbuf = ("pbig", pb)
            b.wait_free("tensor", pbuf)
            b.wait_ready("tensor", wslab)
            b.wait_ready("tensor", "atf")
            mm = None
            for k in range(KD):
                mm = b.emit("tensor", lambda e, p=pbig[pb][:],
                            w=wsb[:, sl * D + k * 128: sl * D + (k + 1) * 128],
                            r=atf[:, k * C + sub * 512: k * C + sub * 512 + 512],
                            st=(k == 0), sp=(k == KD - 1):
                            nc.tensor.matmul(p, w, r, start=st, stop=sp),
                            inc="sT" if k == KD - 1 else None)
            b.produced(pbuf, mm)
            last_mm = mm
            if sub == 1:
                b.consumed(wslab, mm)
            b.wait_ready("vector", pbuf)
            b.wait_ready("vector", "consts")
            m_e = b.emit("vector", lambda e, d=ytmp[:, m * C + sub * 512: m * C + sub * 512 + 512],
                         s=pbig[pb][:], bb=bias16[:, m:m + 1]:
                         nc.vector.tensor_scalar(d, s, bb, None, ALU.add), inc="sV")
            b.consumed(pbuf, m_e)
            evs.append(m_e)
        for m_e in evs:
            b.wait("gpsimd", m_e[0], m_e[1])
        b.wait_free("gpsimd", "yscr")
        m_w = b.emit("gpsimd", lambda e, d=yscr[m * 128:(m + 1) * 128, :],
                     s=ytmp[:, m * C:(m + 1) * C]: e.dma_start(out=d, in_=s),
                     inc="dYw", inc_k=16)
        wr.append(m_w)
        if chunk == 0:
            _gamma_step(b, ctx, 6)
        else:
            _quant_step(b, ctx, 2)
    b.produced("qg", ("dYw", b.pc["dYw"]))
    for f in range(KD):
        b.consumed(("t1f", f), last_mm)
    b.produced("yscr", ("dYw", b.pc["dYw"]))
    ctx["_yscr_marks"] = wr


def _phase_y_ln_resid(b, ctx, chunk):
    nc = b.nc
    yscr, xb, ttok, xbscr = (
        ctx[k] for k in ("yscr", "xb", "ttok", "xbscr"))
    act2 = ctx["act2scr"]
    if chunk == 0:
        ytok, ykn = ctx["ytok"], "xstg"
    else:
        ytok, ykn = ctx["ytokB"], "ytokB"
        b.wait_free("sync", "vg")
    t2wr = []
    b.wait_ready("sync", "yscr")
    for i in range(NTT):
        sl = i % 2
        yt = (ykn, sl)
        b.wait_free("sync", yt)
        m_tr = b.emit("sync", lambda e, d=ytok[:, sl * D:(sl + 1) * D],
                      s=yscr[:, i * 128:(i + 1) * 128]: e.dma_start_transpose(out=d, in_=s),
                      inc=f"dX{sl}", inc_k=16)
        b.produced(yt, m_tr)
        b.consumed("yscr", m_tr)
        b.wait_free("vector", ("ttok", sl))
        m_ap = _ln_stats_apply(b, ctx, ytok[:, sl * D:(sl + 1) * D], yt, i,
                               ttok[:, sl * D:(sl + 1) * D], ("ttok", sl))
        b.wait("vector", "sV", m_ap[1])
        b.wait_ready("vector", ("xb", i))
        m_x = b.emit("vector", lambda e, d=xb[:, i * D:(i + 1) * D],
                     s=xb[:, i * D:(i + 1) * D], t=ttok[:, sl * D:(sl + 1) * D]:
                     nc.vector.tensor_add(d, s, t), inc="sV")
        b.consumed(("ttok", sl), m_x)
        b.produced(("xb", i), m_x)
        b.produced(("xdata", chunk, i), m_x)
        b.wait_free("vector", ("ttok", sl))
        m_t2 = _ln_stats_apply(b, ctx, xb[:, i * D:(i + 1) * D], ("xb", i), i,
                               ttok[:, sl * D:(sl + 1) * D], ("ttok", sl))
        b.wait_ready("gpsimd", ("ttok", sl))
        m_wr = b.emit("gpsimd", lambda e,
                      d=act2[chunk * C + i * 128:chunk * C + (i + 1) * 128, :],
                      s=ttok[:, sl * D:(sl + 1) * D]: e.dma_start(out=d, in_=s),
                      inc=f"dAc2{chunk}_{sl}", inc_k=16)
        b.consumed(("ttok", sl), m_wr)
        t2wr.append(m_wr)
        b.wait("gpsimd", m_t2[0], m_t2[1])
        m_pk = b.emit("gpsimd", lambda e,
                      d=xbscr[chunk * C + i * 128:chunk * C + (i + 1) * 128, :],
                      s=xb[:, i * D:(i + 1) * D]: e.dma_start(out=d, in_=s),
                      inc=f"dPk{chunk}", inc_k=16)
        mjp = b.join("gpsimd", [m_pk])
        b.consumed(("xb", i), mjp)
        if chunk == 0:
            _gamma_step(b, ctx, 1)
    b.produced(("xpark_done", chunk), ("sJ", b.pc["sJ"]))
    ctx[f"_t2wr_{chunk}"] = t2wr
    b.produced(("act2", chunk, 0), (f"dAc2{chunk}_0", b.pc[f"dAc2{chunk}_0"]))
    b.produced(("act2", chunk, 1), (f"dAc2{chunk}_1", b.pc[f"dAc2{chunk}_1"]))


def _gamma_init(b, ctx):
    jobs = []
    for wi, wt in enumerate(("w1", "w2")):
        for s in range(KD):
            for p0 in range(0, FF, 1024):
                jobs.append((wi, wt, s, p0, min(1024, FF - p0)))
    for s in range(MFF):
        for hh in range(2):
            jobs.append((2, "w3", s, hh * 1024, 1024))
    ctx["_gjobs"] = jobs
    ctx["_gidx"] = 0
    ctx["_gcols"] = {}


def _gamma_step(b, ctx, n):
    nc = b.nc
    xstg, xcF, gpart = ctx["xstg"], ctx["xcF"], ctx["gpart"]
    ttokF = (ctx["ttf0"], ctx["ttf1"])
    slots = [(xstg[:, 0:1024], ("xstg", 0)),
             (xstg[:, 1024:2048], ("xstg", 1)),
             (xcF[:, 0:1024], ("xcst",)),
             (ttokF[0][:, 0:1024], ("ttok", 0)),
             (ttokF[1][:, 0:1024], ("ttok", 1))]
    jobs = ctx["_gjobs"]
    for _ in range(n):
        i = ctx["_gidx"]
        if i >= len(jobs):
            return
        ctx["_gidx"] = i + 1
        wi, wname, s, p0, width = jobs[i]
        wsrc = ctx[wname]
        sap, hkey = slots[i % 5]
        b.wait_free("gpsimd", hkey)
        m_l = b.emit("gpsimd", lambda e, d=sap[:, 0:width],
                     s_=wsrc[s * 128:(s + 1) * 128, p0:p0 + width]:
                     e.dma_start(out=d, in_=s_), inc=f"dG{i % 5}", inc_k=16)
        b.produced(hkey, m_l)
        b.wait_ready("scalar", hkey)
        m_a = b.emit("scalar", lambda e, d=sap[:, 0:width],
                     acc=gpart[:, i:i + 1]:
                     nc.scalar.activation(d, d, AF.Abs, accum_out=acc), inc="sS")
        b.consumed(hkey, m_a)
        b.produced(("gpart", i), m_a)
        ctx["_gcols"].setdefault(wi, []).append(i)


def _gamma_final_one(b, ctx, wi):
    nc = b.nc
    gpart, onesf, hgam, lntmp = (
        ctx[k] for k in ("gpart", "onesf", "hgam", "lntmp"))
    pmisc = ctx["pscore"][0]
    cs = ctx["_gcols"][wi]
    lo, hi = min(cs), max(cs) + 1
    assert hi - lo == len(cs) == (88 if wi == 2 else 96), (wi, len(cs))
    b.wait_ready("vector", ("gpart", hi - 1))
    b.wait_free("vector", ("lntmp", wi))
    m_r = b.emit("vector", lambda e, d=lntmp[:, wi:wi + 1], s=gpart[:, lo:hi]:
                 nc.vector.tensor_reduce(d, s, AX.X, ALU.add), inc="sV")
    b.produced(("lntmp", wi), m_r)
    b.produced(("gred", wi), m_r)
    b.wait_free("tensor", ("pscore", 0))
    b.wait_ready("tensor", ("gred", wi))
    b.wait_ready("tensor", "ones")
    m_mm = b.emit("tensor", lambda e, p=pmisc[0:1, wi:wi + 1], o=onesf[:, 0:1],
                  s=lntmp[:, wi:wi + 1]:
                  nc.tensor.matmul(p, o, s, start=True, stop=True), inc="sT")
    b.produced(("gsum", wi), m_mm)
    b.consumed(("lntmp", wi), m_mm)
    b.wait_ready("vector", ("gsum", wi))
    b.wait_free("vector", ("lntmp", 4 + wi))
    m_h = b.emit("vector", lambda e, d=lntmp[0:1, 4 + wi:5 + wi], s=pmisc[0:1, wi:wi + 1]:
                 nc.vector.tensor_scalar(d, s, 0.5 / NW1, 0.5 * EPS_Q, ALU.mult, ALU.add),
                 inc="sV")
    b.consumed(("gsum", wi), m_h)
    b.produced(("hrow", wi), m_h)
    b.produced(("lntmp", 4 + wi), m_h)
    b.wait_ready("tensor", ("hrow", wi))
    m_b = b.emit("tensor", lambda e, p=pmisc[:, 4 + wi:5 + wi], o=onesf[0:1, :],
                 s=lntmp[0:1, 4 + wi:5 + wi]:
                 nc.tensor.matmul(p, o, s, start=True, stop=True), inc="sT")
    b.produced(("hbc", wi), m_b)
    b.consumed(("lntmp", 4 + wi), m_b)
    b.wait_ready("vector", ("hbc", wi))
    m_c = b.emit("vector", lambda e, d=hgam[:, wi:wi + 1], s=pmisc[:, 4 + wi:5 + wi]:
                 nc.vector.tensor_copy(d, s), inc="sV")
    b.consumed(("hbc", wi), m_c)
    m_n2 = b.emit("vector", lambda e, d=hgam[:, 4 + wi:5 + wi], s=pmisc[:, 4 + wi:5 + wi]:
                  nc.vector.tensor_scalar(d, s, -1.0, None, ALU.mult), inc="sV")
    b.consumed(("hbc", wi), m_n2)
    b.consumed(("pscore", 0), m_n2)
    b.produced(("hgam", wi), ("sV", b.pc["sV"]))


def _quant_init(b, ctx):
    jobs = []
    for g in range(3):
        gend = min(2048 * (g + 1), FF)
        for wi, wn, qn in ((0, "w1", "qw1T"), (1, "w2", "qw2T")):
            for si in range(KD):
                for p0 in range(2048 * g, gend, 1024):
                    w = min(1024, gend - p0)
                    last = (si == KD - 1 and p0 + w == gend)
                    jobs.append((wi, ctx[wn], ctx[qn], si, p0, w,
                                 (wi, g) if last else None))
    for si in range(MFF):
        for hh in range(2):
            last = (si == MFF - 1 and hh == 1)
            jobs.append((2, ctx["w3"], ctx["qw3T"], si, hh * 1024, 1024,
                         (2, 0) if last else None))
    ctx["_qjobs"] = jobs
    ctx["_qidx"] = 0
    ctx["_qin"] = 0


def _quant_step(b, ctx, n):
    nc = b.nc
    jobs = ctx.get("_qjobs")
    if not jobs:
        return
    wsf, xstg, xcst, hgam, ttok = (
        ctx[k] for k in ("wsf", "xstg", "xcst", "hgam", "ttok"))
    slots = [(wsf[:, 0:1024], ("wsf", 0)),
             (wsf[:, 1024:2048], ("wsf", 1)),
             (xstg[:, 0:1024], ("xstg", 0)),
             (xstg[:, 1024:2048], ("xstg", 1))]
    N = len(jobs)

    def emit_in(j):
        wi_, wsrc_, _qd, si_, p0_, width_, _g = jobs[j]
        sap_, fsl_ = slots[j % 4]
        b.wait_free("gpsimd", fsl_)
        m = b.emit("gpsimd", lambda e, d=sap_[:, 0:width_],
                   s_=wsrc_[si_ * 128:(si_ + 1) * 128, p0_:p0_ + width_]:
                   e.dma_start(out=d, in_=s_), inc=f"dQf{j % 4}", inc_k=16)
        b.produced(fsl_, m)

    def pump_in(limit):
        while ctx["_qin"] < min(limit, N):
            emit_in(ctx["_qin"])
            ctx["_qin"] += 1

    for _ in range(n):
        idx = ctx["_qidx"]
        if idx >= N:
            return
        pump_in(idx + 4)
        ctx["_qidx"] = idx + 1
        wi, wsrc, qdstT, si, p0, width, grp = jobs[idx]
        sap, fsl = slots[idx % 4]
        b.wait_ready("vector", fsl)
        b.wait_ready("vector", ("hgam", wi))
        b.wait_free("vector", ("xcst",))
        b.wait_free("vector", ("xq", 0))
        m_s = b.emit("vector", lambda e, d=xcst[:, 0:width],
                     s_=sap[:, 0:width], hg=hgam[:, wi:wi + 1]:
                     nc.vector.tensor_scalar(d, s_, hg, None, ALU.is_gt), inc="sV")
        b.produced(("xq", 0), m_s)
        b.wait_free("vector", ("xq", 1))
        m_g = b.emit("vector", lambda e, d=xcst[:, 1024:1024 + width],
                     s_=sap[:, 0:width], nhg=hgam[:, 4 + wi:5 + wi]:
                     nc.vector.tensor_scalar(d, s_, nhg, None, ALU.is_lt), inc="sV")
        b.produced(("xq", 1), m_g)
        b.consumed(fsl, m_g)
        osl = ("ttok", idx % 2)
        b.wait_free("vector", osl)
        b.wait_ready("vector", ("xq", 0))
        b.wait_ready("vector", ("xq", 1))
        m_q = b.emit("vector", lambda e, d=ttok[:, (idx % 2) * D:(idx % 2) * D + width],
                     a=xcst[:, 0:width], gg=xcst[:, 1024:1024 + width]:
                     nc.vector.tensor_sub(d, a, gg), inc="sV")
        b.consumed(("xq", 0), m_q)
        b.consumed(("xq", 1), m_q)
        b.produced(osl, m_q)
        b.wait_ready("gpsimd", osl)
        m_w = b.emit("gpsimd", lambda e, d=qdstT[si * 128:(si + 1) * 128, p0:p0 + width],
                     s_=ttok[:, (idx % 2) * D:(idx % 2) * D + width]:
                     e.dma_start(out=d, in_=s_), inc=f"dQs{idx % 2}", inc_k=16)
        b.consumed(osl, m_w)
        if grp is not None:
            mj = b.join("gpsimd", [(f"dQs{k}", b.pc.get(f"dQs{k}", 0))
                                   for k in range(2)])
            if grp == (2, 0):
                b.produced("qwT3_done_0", mj)
                b.produced("qwT3_done_1", mj)
            else:
                b.produced(("qwTr", grp[0] + 1, grp[1]), mj)


def _ffn_t2f_load(b, ctx, chunk):
    t2f = ctx["t1f"]
    act2 = ctx["act2scr"]
    b.wait_ready("sync", ("act2", chunk, 0))
    b.wait_ready("sync", ("act2", chunk, 1))
    for f in range(KD):
        b.wait_free("sync", ("t1f", f))
        b.emit("sync", lambda e, d=t2f[:, f * C:(f + 1) * C],
               s=act2[chunk * C:(chunk + 1) * C, f * 128:(f + 1) * 128]:
               e.dma_start_transpose(out=d, in_=s),
               inc="dTf", inc_k=16)
    setm = ("dTf", b.pc["dTf"])
    for f in range(KD):
        b.produced(("t1f", f), setm)
    b.produced("t2f", setm)


def _phase_ffn(b, ctx, chunk):
    nc = b.nc
    (t1f, hid, sigst, qw1, qw2, qw3,
     yscr, ytok, ttok, xb, xbscr, ofin, y_out) = (ctx[k] for k in (
        "t1f", "hid", "sigst", "qw1", "qw2", "qw3",
        "yscr", "ytok", "ttok", "xb", "xbscr", "ofin", "y_out"))
    band3 = ctx["band3"]
    bands = [[ctx["bandw1a"], ctx["bandw1b"]], [ctx["bandw2a"], ctx["bandw2b"]]]
    wT = [ctx["qw1T"], ctx["qw2T"]]
    pbig = ctx["pbig4"] + [ctx["pmix"], ctx["pscore"][1]]
    t2f = t1f
    if chunk == 0:
        b.wait_free("tensor", ("pscore", 0))
        b.wait_free("tensor", ("pscore", 1))
        b.wait_free("tensor", ("pmix",))
        for k in ("sexp", "wdiv", "wtb", "vtb"):
            b.wait_free("scalar", (k,))
        for k in ("qg", "kg", "vg"):
            for eng in ("vector", "scalar", "sync", "gpsimd"):
                b.wait_free(eng, k)
        for eng in ("sync", "scalar"):
            b.wait_free(eng, ("wsb", 0))
            b.wait_free(eng, ("wsb", 1))
    if not ctx.get(f"_t2f_loaded_{chunk}"):
        _ffn_t2f_load(b, ctx, chunk)

    NMC = FF // 256
    out_wr = []
    last_w1w2_mm = None
    for mc in range(NMC):
        for wi in range(2):
            ring = "sync"
            if chunk == 0:
                dsem = f"dBS{wi}"
                bkey = ("bandS", wi)
                bandbuf = (ctx["bandw1b"], ctx["bandw2a"])[wi]
            else:
                dsem = f"dB{wi + 1}{mc % 2}"
                bkey = ("band", wi, mc % 2)
                bandbuf = bands[wi][mc % 2]
            b.wait_free(ring, bkey)
            if wi == 0 and mc % 2 == 0:
                b.wait_free(ring, ("band3", 0))
            elif wi == 0:
                b.wait_free(ring, ("band3", 0))
                b.wait_free(ring, ("band3", 1))
            elif mc % 2 == 1:
                b.wait_free(ring, ("band3", 1))
            b.wait_free(ring, ("ofin", 0))
            b.wait_free(ring, ("ofin", 1))
            if chunk == 1:
                b.wait_free(ring, ("wsf", 0))
                b.wait_free(ring, ("wsf", 1))
                b.wait_free(ring, ("xstg", 0))
                b.wait_free(ring, ("xstg", 1))
            if chunk == 0 and wi == 0:
                b.wait_free(ring, ("wsb", 0))
                b.wait_free(ring, ("wsb", 1))
            b.wait_ready(ring, ("qwTr", wi + 1, min(2, (mc * 256) // 2048)))
            src_r = wT[wi].rearrange("(k p) m -> p k m", p=128)
            mlast = None
            for g2 in range(2):
                mlast = b.emit(ring, lambda e,
                               d=bandbuf[:].rearrange("p (k m) -> p k m", m=256)[:, g2 * 8:(g2 + 1) * 8, :],
                               s=src_r[:, g2 * 8:(g2 + 1) * 8, mc * 256:(mc + 1) * 256]:
                               e.dma_start(out=d, in_=s), inc=dsem, inc_k=16)
            b.produced(bkey, mlast)
            for mt in range(2):
                for sub in range(2):
                    pb = (mt * 2 + sub) % 3 + 3 * wi
                    pbuf = ("pbig", pb)
                    b.wait_free("tensor", pbuf)
                    b.wait_ready("tensor", bkey)
                    b.wait_ready("tensor", "t2f")
                    mm = None
                    for k in range(KD):
                        mm = b.emit("tensor", lambda e, p=pbig[pb][:],
                                    w=bandbuf[:, k * 256 + mt * 128: k * 256 + (mt + 1) * 128],
                                    r=t2f[:, k * C + sub * 512: k * C + sub * 512 + 512],
                                    st=(k == 0), sp=(k == KD - 1):
                                    nc.tensor.matmul(p, w, r, start=st, stop=sp),
                                    inc="sT" if k == KD - 1 else None)
                    b.produced(pbuf, mm)
                    last_w1w2_mm = mm
                    if mt == 1 and sub == 1:
                        b.consumed(bkey, mm)
                    scol = (mt * 2 + sub) * 512
                    if wi == 0:
                        b.wait_ready("scalar", pbuf)
                        b.wait_free("scalar", ("sigst", mt * 2 + sub))
                        m_sg = b.emit("scalar", lambda e, d=sigst[:, scol:scol + 512],
                                      s=pbig[pb][:]: nc.scalar.activation(d, s, AF.Sigmoid),
                                      inc="sS")
                        b.consumed(pbuf, m_sg)
                        b.produced(("sigst", mt * 2 + sub), m_sg)
                    else:
                        b.wait_ready("vector", pbuf)
                        b.wait_ready("vector", ("sigst", mt * 2 + sub))
                        hcol = (mc * 2 + mt) * C + sub * 512
                        b.wait_free("vector", ("hid", mc * 2 + mt))
                        if mc * 2 + mt == 32:
                            b.wait_free("vector", ("ytokB", 0))
                            b.wait_free("vector", ("ytokB", 1))
                        if mc * 2 + mt == 36:
                            b.wait_free("vector", ("ofinC0", 0))
                            b.wait_free("vector", ("ofinC0", 1))
                        m_h = b.emit("vector", lambda e, d=hid[:, hcol:hcol + 512],
                                     a=sigst[:, scol:scol + 512], p=pbig[pb][:]:
                                     nc.vector.tensor_mul(d, a, p), inc="sV")
                        b.consumed(pbuf, m_h)
                        b.consumed(("sigst", mt * 2 + sub), m_h)
                        if sub == 1:
                            b.produced(("hid", mc * 2 + mt), m_h)
                        else:
                            b.produced(("hidA", mc * 2 + mt), m_h)
    for f in range(KD):
        b.consumed(("t1f", f), last_w1w2_mm)
    NDC = D // 128
    src3 = ctx["qw3T"].rearrange("(k p) m -> p k m", p=128)
    for dc in range(NDC):
        b3 = ("band3", dc % 2)
        ring, dsem = (("sync", "dB3a") if dc % 2 == 0 else ("sync", "dB3b"))
        b.wait_free(ring, b3)
        b.wait_free(ring, ("band", 0, 0))
        b.wait_free(ring, ("band", 0, 1))
        b.wait_free(ring, ("band", 1, 0))
        b.wait_free(ring, ("band", 1, 1))
        b.wait_free(ring, ("bandS", 0))
        b.wait_free(ring, ("ofin", 0))
        b.wait_free(ring, ("ofin", 1))
        b.wait_free(ring, ("xstg", 0))
        b.wait_free(ring, ("xstg", 1))
        b.wait_ready(ring, "qwT3_done_0")
        b.wait_ready(ring, "qwT3_done_1")
        base3 = (dc % 2) * MFF * 128
        mlast = None
        for g4 in range(4):
            mlast = b.emit(ring, lambda e,
                           d=band3[:, base3:base3 + MFF * 128].rearrange("p (k m) -> p k m", m=128)[:, g4 * 11:(g4 + 1) * 11, :],
                           s=src3[:, g4 * 11:(g4 + 1) * 11, dc * 128:(dc + 1) * 128]:
                           e.dma_start(out=d, in_=s), inc=dsem, inc_k=16)
        b.produced(b3, mlast)
        if chunk == 0 and dc == 1:
            _ffn_t2f_load(b, ctx, 1)
            ctx["_t2f_loaded_1"] = True
        for sub in range(2):
            mg = dc * 2 + sub
            pb = mg % 4
            pbuf = ("pbig", pb)
            b.wait_free("tensor", pbuf)
            b.wait_ready("tensor", b3)
            for k in range(MFF):
                b.wait_ready("tensor", ("hid", k))
            mm = None
            for k in range(MFF):
                mm = b.emit("tensor", lambda e, p=pbig[pb][:],
                            w=band3[:, base3 + k * 128:base3 + (k + 1) * 128],
                            r=hid[:, k * C + sub * 512: k * C + sub * 512 + 512],
                            st=(k == 0), sp=(k == MFF - 1):
                            nc.tensor.matmul(p, w, r, start=st, stop=sp),
                            inc="sT" if k == MFF - 1 else None)
            b.produced(pbuf, mm)
            if sub == 1:
                b.consumed(b3, mm)
            if dc == NDC - 1 and sub == 1:
                for k in range(MFF):
                    b.consumed(("hid", k), mm)
            b.wait_ready("vector", pbuf)
            osl = ("ttok", mg % 2)
            b.wait_free("vector", osl)
            m_e = b.emit("vector", lambda e, d=ttok[:, (mg % 2) * D:(mg % 2) * D + 512],
                         s=pbig[pb][:]: nc.vector.tensor_copy(d, s), inc="sV")
            b.consumed(pbuf, m_e)
            b.produced(osl, m_e)
            b.wait_ready("gpsimd", osl)
            b.wait_free("gpsimd", "yscr")
            m_w = b.emit("gpsimd", lambda e,
                         d=yscr[dc * 128:(dc + 1) * 128, sub * 512:sub * 512 + 512],
                         s=ttok[:, (mg % 2) * D:(mg % 2) * D + 512]:
                         e.dma_start(out=d, in_=s), inc="dYw", inc_k=16)
            mj = b.join("gpsimd", [m_w])
            b.consumed(osl, mj)
            out_wr.append(m_w)
    b.produced("yscr", ("dYw", b.pc["dYw"]))
    if chunk == 0:
        ytl, ytkn = ctx["ytokB"], "ytokB"
        ofb, ofkn = ctx["ofinC0"], "ofinC0"
    else:
        ytl, ytkn = ctx["ytok"], "xstg"
        ofb, ofkn = ctx["ofin"], "ofin"
    for i in range(NTT):
        b.wait_ready("scalar", ("xpark_done", chunk))
        b.wait_free("scalar", ("xb", i))
        b.emit("scalar", lambda e, d=xb[:, i * D:(i + 1) * D],
               s=xbscr[chunk * C + i * 128:chunk * C + (i + 1) * 128, :]:
               e.dma_start(out=d, in_=s), inc=f"dRs{chunk}", inc_k=16)
    m_xr = (f"dRs{chunk}", b.pc[f"dRs{chunk}"])
    for i in range(NTT):
        b.produced(("xb", i), m_xr)
    b.wait_ready("sync", "yscr")
    for i in range(NTT):
        sl = i % 2
        yt = (ytkn, sl)
        b.wait_free("sync", yt)
        if chunk == 1:
            b.wait_free("sync", ("band3", 0))
            b.wait_free("sync", ("band3", 1))
            b.wait_free("sync", ("band", 1, 1))
        m_tr = b.emit("sync", lambda e, d=ytl[:, sl * D:(sl + 1) * D],
                      s=yscr[:, i * 128:(i + 1) * 128]: e.dma_start_transpose(out=d, in_=s),
                      inc=f"dX{sl}", inc_k=16)
        b.produced(yt, m_tr)
        b.consumed("yscr", m_tr)
        b.wait_free("vector", ("ttok", sl))
        m_ap = _ln_stats_apply(b, ctx, ytl[:, sl * D:(sl + 1) * D], yt, i,
                               ttok[:, sl * D:(sl + 1) * D], ("ttok", sl))
        b.wait("vector", "sV", m_ap[1])
        b.wait_ready("vector", ("xb", i))
        osl = (ofkn, sl)
        b.wait_free("vector", osl)
        if chunk == 1:
            b.wait_free("vector", ("band3", 0))
            b.wait_free("vector", ("band3", 1))
            b.wait_free("vector", ("band", 0, 0))
            b.wait_free("vector", ("band", 0, 1))
            b.wait_free("vector", ("wsf", 0))
            b.wait_free("vector", ("wsf", 1))
            b.wait_free("vector", ("wsb", 0))
            b.wait_free("vector", ("wsb", 1))
        m_o = b.emit("vector", lambda e, d=ofb[:, sl * D:(sl + 1) * D],
                     s=xb[:, i * D:(i + 1) * D], t=ttok[:, sl * D:(sl + 1) * D]:
                     nc.vector.tensor_add(d, s, t), inc="sV")
        b.consumed(("ttok", sl), m_o)
        b.consumed(("xb", i), m_o)
        b.produced(osl, m_o)
        b.wait_ready("gpsimd", osl)
        m_out = b.emit("gpsimd", lambda e,
                       d=y_out[chunk * C + i * 128:chunk * C + (i + 1) * 128, :],
                       s=ofb[:, sl * D:(sl + 1) * D]: e.dma_start(out=d, in_=s),
                       inc="dO", inc_k=16)
        mj = b.join("gpsimd", [m_out])
        b.consumed(osl, mj)


_NC_CACHE = {}


def _get_program(dbg=None):
    if dbg not in _NC_CACHE:
        _NC_CACHE[dbg] = build_program(dbg)
    return _NC_CACHE[dbg]


def _make_consts():
    ident = np.eye(128, dtype=bfdt)
    mask = np.zeros((128, 512), dtype=bfdt)
    for gg in range(4):
        for t in range(8):
            mask[t * 16:(t + 1) * 16, gg * 128 + t * 16:gg * 128 + (t + 1) * 16] = 1
    return ident, mask


def kernel(L0_attn_x, L0_attn_W_qkv, L0_attn_W_out, L0_attn_b_out,
           L0_ffn_w1, L0_ffn_w2, L0_ffn_w3):
    x = np.ascontiguousarray(np.asarray(L0_attn_x, dtype=np.float32).reshape(16384, 2048))
    ident, mask = _make_consts()
    bout_t = np.ascontiguousarray(np.asarray(L0_attn_b_out, dtype=np.float32).reshape(16, 128).T)
    w1t = np.ascontiguousarray(np.asarray(L0_ffn_w1, dtype=np.float32).T)
    w2t = np.ascontiguousarray(np.asarray(L0_ffn_w2, dtype=np.float32).T)
    w3t = np.ascontiguousarray(np.asarray(L0_ffn_w3, dtype=np.float32).T)
    nc, b = _get_program(DEBUG_STAGE)
    in_maps = []
    for c in range(NCORES):
        in_maps.append(dict(
            x=x[c * T:(c + 1) * T],
            wqkv=np.asarray(L0_attn_W_qkv, dtype=np.float32),
            wout=np.asarray(L0_attn_W_out, dtype=np.float32),
            bout=bout_t,
            w1=w1t, w2=w2t, w3=w3t,
            ident=ident, mask01=mask,
        ))
    res = run_bass_kernel_spmd(nc, in_maps, core_ids=list(range(NCORES)))
    out = np.concatenate([r["y"] for r in res.results], axis=0)
    return out.reshape(4, 4096, 2048)


# version 46
# speedup vs baseline: 1.1415x; 1.1415x over previous
import numpy as np
import ml_dtypes
import concourse.bass as bass
import concourse.mybir as mybir
from concourse.bass_utils import run_bass_kernel_spmd

F32 = mybir.dt.float32
BF16 = mybir.dt.bfloat16
AF = mybir.ActivationFunctionType
ALU = mybir.AluOpType
AX = mybir.AxisListType
bfdt = ml_dtypes.bfloat16

NCORES = 8
T = 2048
C = 1024
NTT = C // 128
D = 2048
KD = D // 128
NH = 16
QKV = 3 * D
MQKV = QKV // 128
FF = 5632
MFF = FF // 128
NG = C // 8
NB = NG // 4
EPS_LN = 1e-5
EPS_Q = 1e-8
QSCALE = float(1.0 / np.sqrt(128.0))
NW1 = FF * D

DEBUG_STAGE = None
ATTN_PARTS = 5
ENGINES = ("sync", "scalar", "vector", "tensor", "gpsimd")


class Builder:
    def __init__(self, nc):
        self.nc = nc
        self.streams = {k: [] for k in ENGINES}
        self.meta = {k: [] for k in ENGINES}
        self.pc = {}
        self.waited = {k: {} for k in ENGINES}
        self.sems = {}
        self.ready = {}
        self.cons = {}
        self.dbg_outs = {}

    def sem(self, name):
        if name not in self.sems:
            self.sems[name] = self.nc.semaphore(name).__enter__()
            self.pc[name] = 0
        return self.sems[name]

    def emit(self, eng, fn, inc=None, inc_k=1):
        if inc is not None:
            self.sem(inc)
            self.pc[inc] += inc_k
            val = self.pc[inc]
            sem = self.sems[inc]

            def wrapped(e, fn=fn, sem=sem, k=inc_k):
                fn(e).then_inc(sem, k)
            self.streams[eng].append(wrapped)
            self.meta[eng].append(("i", inc, inc_k))
            return (inc, val)
        self.streams[eng].append(fn)
        self.meta[eng].append(("o",))
        return None

    def wait(self, eng, sem_name, val):
        if val is None or val <= 0:
            return
        w = self.waited[eng]
        if w.get(sem_name, 0) >= val:
            return
        w[sem_name] = val
        sem = self.sem(sem_name)
        self.streams[eng].append(lambda e, s=sem, v=val: e.wait_ge(s, v))
        self.meta[eng].append(("w", sem_name, val))

    def produced(self, buf, mark):
        self.ready[buf] = mark
        self.cons[buf] = []

    def consumed(self, buf, mark):
        self.cons.setdefault(buf, []).append(mark)

    def wait_ready(self, eng, buf):
        m = self.ready.get(buf)
        if m:
            self.wait(eng, m[0], m[1])

    def wait_free(self, eng, buf):
        for m in self.cons.get(buf, ()):
            self.wait(eng, m[0], m[1])
        m = self.ready.get(buf)
        if m:
            self.wait(eng, m[0], m[1])

    def join(self, eng, marks, tag="sJ"):
        for sm, v in marks:
            self.wait(eng, sm, v)
        self.sem(tag)
        self.pc[tag] += 1
        val = self.pc[tag]
        sem = self.sems[tag]
        self.streams[eng].append(lambda e, s=sem: e.sem_inc(s, 1))
        self.meta[eng].append(("i", tag, 1))
        return (tag, val)

    def check_deadlock(self):
        pos = {k: 0 for k in ENGINES}
        cnt = {}
        progress = True
        while progress:
            progress = False
            for eng in ENGINES:
                ms = self.meta[eng]
                while pos[eng] < len(ms):
                    m = ms[pos[eng]]
                    if m[0] == "w" and cnt.get(m[1], 0) < m[2]:
                        break
                    if m[0] == "i":
                        cnt[m[1]] = cnt.get(m[1], 0) + m[2]
                    pos[eng] += 1
                    progress = True
        stuck = {e: (pos[e], len(self.meta[e])) for e in ENGINES if pos[e] < len(self.meta[e])}
        if stuck:
            detail = {e: self.meta[e][pos[e]] for e in stuck}
            raise RuntimeError(f"deadlock in static schedule: {stuck} blocked at {detail}")


def build_program(dbg_stage=None):
    nc = bass.Bass("TRN2", target_bir_lowering=False)
    b = Builder(nc)
    b.dbg = dbg_stage

    x_in = nc.dram_tensor("x", [T, D], F32, kind="ExternalInput")
    wqkv = nc.dram_tensor("wqkv", [D, QKV], F32, kind="ExternalInput")
    wout = nc.dram_tensor("wout", [D, D], F32, kind="ExternalInput")
    bout = nc.dram_tensor("bout", [128, KD], F32, kind="ExternalInput")
    w1 = nc.dram_tensor("w1", [D, FF], F32, kind="ExternalInput")
    w2 = nc.dram_tensor("w2", [D, FF], F32, kind="ExternalInput")
    w3 = nc.dram_tensor("w3", [FF, D], F32, kind="ExternalInput")
    ident_in = nc.dram_tensor("ident", [128, 128], BF16, kind="ExternalInput")
    mask_in = nc.dram_tensor("mask01", [128, 512], BF16, kind="ExternalInput")
    y_out = nc.dram_tensor("y", [T, D], F32, kind="ExternalOutput")

    actscr = nc.dram_tensor("actscr", [C, D], BF16, kind="Internal")
    act2scr = nc.dram_tensor("act2scr", [T, D], BF16, kind="Internal")
    yscr = nc.dram_tensor("yscr", [D, C], BF16, kind="Internal")
    xbscr = nc.dram_tensor("xbscr", [T, D], BF16, kind="Internal")
    wqkvb = nc.dram_tensor("wqkvb", [MQKV, 128, D], BF16, kind="Internal")
    woutb = nc.dram_tensor("woutb", [KD, 128, D], BF16, kind="Internal")
    qw1 = nc.dram_tensor("qw1", [FF, D], BF16, kind="Internal")
    qw2 = nc.dram_tensor("qw2", [FF, D], BF16, kind="Internal")
    qw3 = nc.dram_tensor("qw3", [D, FF], BF16, kind="Internal")
    qw1T = nc.dram_tensor("qw1T", [D, FF], BF16, kind="Internal")
    qw2T = nc.dram_tensor("qw2T", [D, FF], BF16, kind="Internal")
    qw3T = nc.dram_tensor("qw3T", [FF, D], BF16, kind="Internal")

    BA = 16512
    AT = lambda name, shape, dt, off: nc.alloc_sbuf_tensor_at(name, shape, dt, offset=BA + off)
    xb = AT("xb", [128, NTT * D], BF16, 0)
    ident = AT("identS", [128, 128], BF16, 32768)
    mask01 = AT("mask01S", [128, 512], BF16, 33024)
    bias16 = AT("bias16", [128, KD], F32, 34048)
    onesf = AT("onesf", [128, 128], F32, 34112)
    stats = AT("stats", [128, 24], F32, 34624)
    mv = AT("mv", [128, 16], F32, 34720)
    rstd8 = AT("rstd8", [128, 8], F32, 34784)
    nbias8 = AT("nbias8", [128, 8], F32, 34816)
    lntmp = AT("lntmp", [128, 8], F32, 34848)
    hgam = AT("hgam", [128, 8], F32, 34880)
    sums4 = AT("sums4", [128, 16], F32, 34912)
    gpart = AT("gpart", [128, 384], F32, 34976)
    epsc = AT("epsc", [128, 4], F32, 36512)
    sexp = AT("sexp", [128, 1024], BF16, 36544)
    wdiv = AT("wdiv", [128, 1024], BF16, 38592)
    wtb = AT("wtb", [128, 1024], BF16, 40640)
    vtb = AT("vtb", [128, 1024], BF16, 42688)
    sigst = AT("sigst", [128, 4096], BF16, 36544)
    ttok = AT("ttok", [128, 2 * D], BF16, 44736)
    ttf0 = AT("ttf0", [128, 1024], F32, 44736)
    ttf1 = AT("ttf1", [128, 1024], F32, 48832)
    qg = AT("qg", [128, NG * 128], BF16, 52928)
    kg = AT("kg", [128, NG * 128], BF16, 85696)
    vg = AT("vg", [128, NG * 128], BF16, 118464)
    hid = AT("hid", [128, MFF * C], BF16, 52928)
    ytokB = AT("ytokB", [128, 2 * D], BF16, 118464)
    ofinC0 = AT("ofinC0", [128, 2 * D], F32, 126656)
    bandw1a = AT("bandw1a", [128, KD * 256], BF16, 184000)
    bandw1b = AT("bandw1b", [128, KD * 256], BF16, 192192)
    bandw2a = AT("bandw2a", [128, KD * 256], BF16, 143040)
    bandw2b = AT("bandw2b", [128, KD * 256], BF16, 200384)
    band3 = AT("band3", [128, 2 * MFF * 128], BF16, 184000)
    t1f = AT("t1f", [128, KD * C], BF16, 151232)
    wsf = AT("wsf", [128, 2048], F32, 184000)
    wsfq0 = AT("wsfq0", [128, 1024], BF16, 184000)
    wsfq1 = AT("wsfq1", [128, 1024], BF16, 188096)
    wsb = AT("wsb", [128, 2 * D], BF16, 192192)
    xstg = AT("xstg", [128, 2048], F32, 200384)
    xstq0 = AT("xstq0", [128, 1024], BF16, 200384)
    xstq1 = AT("xstq1", [128, 1024], BF16, 204480)
    ytok = AT("ytok", [128, 2 * D], BF16, 200384)
    xcst = AT("xcst", [128, 2048], BF16, 208576)
    xcF = AT("xcF", [128, 1024], F32, 208576)
    ofin = AT("ofin", [128, 2 * D], F32, 184000)

    from contextlib import ExitStack
    with ExitStack() as st:
        e = st.enter_context
        pbig = [e(nc.psum_tensor(f"pbig{i}", [128, 512], F32)) for i in range(3)]
        pscore = [e(nc.psum_tensor(f"pscore{i}", [128, 512], F32)) for i in range(2)]
        pmix = e(nc.psum_tensor("pmix", [128, 512], F32))
        pta = e(nc.psum_tensor("pta", [128, 1024], BF16))
        ptb = e(nc.psum_tensor("ptb", [128, 1024], BF16))
        ptx = [pta, ptb]
        pbig4 = pbig + [pscore[0]]
        pmisc = pscore[0]
        blocks = e(nc.Block())

        ctx = dict(locals())
        _build_all(b, ctx)
        b.check_deadlock()

        blocks.sync(lambda eng: [fn(eng) for fn in b.streams["sync"]])
        blocks.scalar(lambda eng: [fn(eng) for fn in b.streams["scalar"]])
        blocks.vector(lambda eng: [fn(eng) for fn in b.streams["vector"]])
        blocks.tensor(lambda eng: [fn(eng) for fn in b.streams["tensor"]])
        blocks.gpsimd(lambda eng: [fn(eng) for fn in b.streams["gpsimd"]])

    return nc, b


def _build_all(b, ctx):
    nc = b.nc
    g = lambda n: ctx[n]

    b.emit("sync", lambda e, a=g("ident"), s=g("ident_in"): e.dma_start(out=a[:], in_=s[:, :]),
           inc="dC", inc_k=16)
    b.emit("sync", lambda e, a=g("mask01"), s=g("mask_in"): e.dma_start(out=a[:], in_=s[:, :]),
           inc="dC", inc_k=16)
    b.emit("sync", lambda e, a=g("bias16"), s=g("bout"): e.dma_start(out=a[:], in_=s[:, :]),
           inc="dC", inc_k=16)
    b.produced("consts", ("dC", b.pc["dC"]))
    b.emit("vector", lambda e, a=g("onesf"): nc.vector.memset(a[:], 1.0), inc="sV")
    b.produced("ones", ("sV", b.pc["sV"]))
    b.emit("vector", lambda e, a=g("lntmp"): nc.vector.memset(a[:], 0.0), inc="sV")
    b.produced(("lntmp", 3), ("sV", b.pc["sV"]))
    b.emit("vector", lambda e, a=g("epsc"): nc.vector.memset(a[:], EPS_LN), inc="sV")
    b.produced("epsc", ("sV", b.pc["sV"]))

    _gamma_init(b, ctx)
    _quant_init(b, ctx)

    _phase_xload(b, ctx, chunk=0, precast=False)
    _phase_ln_t(b, ctx, chunk=0)
    if b.dbg == "t1":
        _dump(b, ctx, "t1f", g("t1f")); return
    _phase_xload(b, ctx, chunk=1, precast=True)
    _phase_qkv(b, ctx, chunk=0)
    _gamma_final_one(b, ctx, 0)
    if b.dbg == "qkv":
        for n in ("qg", "kg", "vg"):
            _dump(b, ctx, n, g(n))
        return
    _phase_attn_inner(b, ctx, chunk=0)
    _gamma_final_one(b, ctx, 1)
    if b.dbg == "attn":
        _dump(b, ctx, "atf", g("t1f")); return
    _phase_wout(b, ctx, chunk=0)
    _gamma_final_one(b, ctx, 2)
    _phase_y_ln_resid(b, ctx, chunk=0)
    if b.dbg == "xp":
        _dump(b, ctx, "xb", g("xb")); return

    _phase_xread_bf16(b, ctx, chunk=1)
    _phase_ln_t(b, ctx, chunk=1)
    _phase_qkv(b, ctx, chunk=1)
    _phase_attn_inner(b, ctx, chunk=1)
    _phase_wout(b, ctx, chunk=1)
    _ffn_t2f_load(b, ctx, 0)
    ctx["_t2f_loaded_0"] = True
    _quant_step(b, ctx, 10 ** 9)
    _phase_y_ln_resid(b, ctx, chunk=1)
    if b.dbg == "xp2":
        _dump(b, ctx, "xb", g("xb")); return
    if b.dbg == "quant":
        _dump_dram(b, ctx, "qw1", g("qw1")); return

    if b.dbg == "qwt":
        for n in ("qw1T", "qw2T", "qw3T"):
            _dump_dram(b, ctx, n, g(n))
        return
    for chunk in range(2):
        _phase_ffn(b, ctx, chunk)
        if b.dbg == "of":
            _dump_dram(b, ctx, "yscr", g("yscr")); return
        if b.dbg == "hid":
            _dump(b, ctx, "hid", g("hid")); return
        if b.dbg == "t2f":
            _dump(b, ctx, "t2f", g("t1f")); return
    b.wait("sync", "dO", b.pc["dO"])
    b.wait("gpsimd", "dO", b.pc["dO"])


def _dump(b, ctx, name, sb):
    nc = b.nc
    o = nc.dram_tensor("dbg_" + name, [sb.shape[0], sb.shape[1]], sb.dtype,
                       kind="ExternalOutput")
    for s in sorted(b.pc):
        if s != "dO":
            b.wait("gpsimd", s, b.pc[s])
    b.emit("gpsimd", lambda e, o=o, sb=sb: e.dma_start(out=o[:, :], in_=sb[:]),
           inc="dO", inc_k=16)
    b.wait("gpsimd", "dO", b.pc["dO"])
    b.dbg_outs[name] = o


def _dump_dram(b, ctx, name, dr):
    nc = b.nc
    o = nc.dram_tensor("dbg_" + name, [dr.shape[0], dr.shape[1]], dr.dtype,
                       kind="ExternalOutput")
    for s in sorted(b.pc):
        if s != "dO":
            b.wait("gpsimd", s, b.pc[s])
    b.emit("gpsimd", lambda e, o=o, dr=dr: e.dma_start(out=o[:, :], in_=dr[:, :]),
           inc="dO", inc_k=16)
    b.wait("gpsimd", "dO", b.pc["dO"])
    b.dbg_outs[name] = o


def _phase_xload(b, ctx, chunk, precast):
    nc = b.nc
    x_in, xb, xbscr = (ctx[k] for k in ("x_in", "xb", "xbscr"))
    if not precast:
        for i in range(NTT):
            b.wait_free("gpsimd", ("xb", i))
            b.emit("gpsimd", lambda e, d=xb[:, i * D:(i + 1) * D],
                   s=x_in[chunk * C + i * 128:chunk * C + (i + 1) * 128, :]:
                   e.dma_start(out=d, in_=s), inc=f"dXc{i}", inc_k=16)
            m = (f"dXc{i}", b.pc[f"dXc{i}"])
            b.produced(("xb", i), m)
            b.produced(("xdata", chunk, i), m)
    else:
        b.emit("gpsimd", lambda e, d=xbscr[chunk * C:(chunk + 1) * C, :],
               s=x_in[chunk * C:(chunk + 1) * C, :]:
               e.dma_start(out=d, in_=s), inc="dPr", inc_k=16)
        b.produced("x1pre", ("dPr", b.pc["dPr"]))


def _phase_xread_bf16(b, ctx, chunk):
    xb, xbscr = ctx["xb"], ctx["xbscr"]
    b.wait_ready("sync", "x1pre")
    for i in range(NTT):
        b.wait_free("sync", ("xb", i))
        b.emit("sync", lambda e, d=xb[:, i * D:(i + 1) * D],
               s=xbscr[chunk * C + i * 128:chunk * C + (i + 1) * 128, :]:
               e.dma_start(out=d, in_=s), inc=f"dXr{i}", inc_k=16)
        m = (f"dXr{i}", b.pc[f"dXr{i}"])
        b.produced(("xb", i), m)
        b.produced(("xdata", chunk, i), m)


def _ln_stats_apply(b, ctx, src_tile_ap, src_key, i, out_ap, out_key):
    nc = b.nc
    stats, mv, rstd8, nbias8, lntmp = (ctx[k] for k in ("stats", "mv", "rstd8", "nbias8", "lntmp"))
    b.wait_free("vector", ("stats",))
    if src_key is not None:
        b.wait_ready("vector", src_key)
    mm = None
    for j in range(4):
        mm = b.emit("vector", lambda e, d=stats[:, j * 6:(j + 1) * 6],
                    s=src_tile_ap[:, j * 512:(j + 1) * 512]: nc.vector.bn_stats(d, s),
                    inc="sV" if j == 3 else None)
    b.produced(("stats",), mm)
    b.wait_ready("vector", ("stats",))
    b.wait_free("vector", ("mv", i))
    m_ag = b.emit("vector", lambda e, d=mv[:, 2 * i:2 * i + 2], s=stats[:]:
                  nc.vector.bn_aggr(d, s), inc="sV")
    b.consumed(("stats",), m_ag)
    b.produced(("mv", i), m_ag)
    b.wait_ready("scalar", ("mv", i))
    b.wait_ready("scalar", "epsc")
    b.wait_free("scalar", ("lntmp", i))
    epsc = ctx["epsc"]
    m_ln = b.emit("scalar", lambda e, d=lntmp[:, i:i + 1], s=mv[:, 2 * i + 1:2 * i + 2],
                  bb=epsc[:, 0:1]:
                  nc.scalar.activation(d, s, AF.Ln, bias=bb), inc="sS")
    b.produced(("lntmp", i), m_ln)
    b.wait("scalar", "sS", m_ln[1])
    b.wait_free("scalar", ("rstd", i))
    m_rs = b.emit("scalar", lambda e, d=rstd8[:, i:i + 1], s=lntmp[:, i:i + 1]:
                  nc.scalar.activation(d, s, AF.Exp, scale=-0.5), inc="sS")
    b.consumed(("lntmp", i), m_rs)
    b.produced(("rstd", i), m_rs)
    b.wait_ready("vector", ("rstd", i))
    b.wait_free("vector", ("nbias", i))
    m_nb = b.emit("vector", lambda e, d=nbias8[:, i:i + 1], s=mv[:, 2 * i:2 * i + 1],
                  r=rstd8[:, i:i + 1]: nc.vector.tensor_scalar(d, s, r, -1.0, ALU.mult, ALU.mult),
                  inc="sV")
    b.consumed(("mv", i), m_nb)
    b.produced(("nbias", i), m_nb)
    b.wait_ready("vector", ("nbias", i))
    m_ap = b.emit("vector", lambda e, d=out_ap, s=src_tile_ap,
                  r=rstd8[:, i:i + 1], nb=nbias8[:, i:i + 1]:
                  nc.vector.tensor_scalar(d, s, r, nb, ALU.mult, ALU.add), inc="sV")
    b.consumed(("rstd", i), m_ap)
    b.consumed(("nbias", i), m_ap)
    if src_key is not None:
        b.consumed(src_key, m_ap)
    b.produced(out_key, m_ap)
    return m_ap


def _phase_ln_t(b, ctx, chunk):
    nc = b.nc
    xb, ttok, t1f, ident = (ctx[k] for k in ("xb", "ttok", "t1f", "ident"))
    ptx = ctx["ptx"]
    if chunk == 0:
        b.wait_ready("tensor", "consts")
    last_ev = None
    for i in range(NTT):
        sl = i % 2
        b.wait_free("vector", ("ttok", sl))
        _ln_stats_apply(b, ctx, xb[:, i * D:(i + 1) * D], ("xb", i), i,
                        ttok[:, sl * D:(sl + 1) * D], ("ttok", sl))
        if i == 0:
            for f in range(KD):
                b.wait_free("tensor", ("t1f", f))
        mt_last = None
        for k8 in range(2):
            pkey = ("ptbank", k8)
            pbank = ptx[k8]
            b.wait_free("tensor", pkey)
            b.wait_ready("tensor", ("ttok", sl))
            for kk in range(8):
                k = k8 * 8 + kk
                mt_last = b.emit("tensor", lambda e, p=pbank[:, kk * 128:(kk + 1) * 128],
                                 s=ttok[:, sl * D + k * 128: sl * D + (k + 1) * 128],
                                 idn=ident[:]:
                                 nc.tensor.transpose(p, s, idn),
                                 inc="sT" if kk == 7 else None)
            b.produced(pkey, mt_last)
            b.wait_ready("vector", pkey)
            dst = t1f[:, :].rearrange("p (k c) -> p k c", c=C)[
                :, k8 * 8:(k8 + 1) * 8, i * 128:(i + 1) * 128]
            m_e = b.emit("vector", lambda e, d=dst,
                         s=pbank[:].rearrange("p (k f) -> p k f", k=8):
                         nc.vector.tensor_copy(d, s), inc="sV")
            b.consumed(pkey, m_e)
            last_ev = m_e
        b.consumed(("ttok", sl), mt_last)
        _gamma_step(b, ctx, 1)
    for f in range(KD):
        b.produced(("t1f", f), last_ev)
    b.produced("t1f", last_ev)


def _stream_wslab(b, ctx, chunk, wsrc, wbscr, m):
    nc = b.nc
    wsf, wsb = ctx["wsf"], ctx["wsb"]
    sl = m % 2
    wslab = ("wsb", sl)
    if chunk == 0:
        half_marks = []
        for q4 in range(4):
            fslq = ("wsf", q4 // 2)
            if q4 % 2 == 0:
                b.wait_free("sync", fslq)
            dst = wsf[:, q4 * 512:(q4 + 1) * 512].rearrange("p (k c) -> p k c", k=4)
            src = wsrc[q4 * 512:(q4 + 1) * 512, m * 128:(m + 1) * 128].rearrange(
                "(k p) c -> p k c", p=128)
            qm = b.emit("sync", lambda e, d=dst, s=src: e.dma_start(out=d, in_=s),
                        inc=f"dWf{q4 // 2}", inc_k=16)
            if q4 % 2 == 1:
                b.produced(fslq, qm)
        b.wait_free("vector", wslab)
        m_c = None
        for hh in range(2):
            b.wait_ready("vector", ("wsf", hh))
            m_c = b.emit("vector", lambda e, d=wsb[:, sl * D + hh * 1024: sl * D + (hh + 1) * 1024],
                         s=wsf[:, hh * 1024:(hh + 1) * 1024]:
                         nc.vector.tensor_copy(d, s), inc="sV")
            b.consumed(("wsf", hh), m_c)
        b.produced(wslab, m_c)
        b.wait_ready("gpsimd", wslab)
        pref = "dWq" if wbscr.name.startswith("wqkvb") else "dWo"
        m_st = b.emit("gpsimd", lambda e, d=wbscr[m, :, :], s=wsb[:, sl * D:(sl + 1) * D]:
                      e.dma_start(out=d, in_=s), inc=f"{pref}{sl}", inc_k=16)
        b.consumed(wslab, m_st)
        if m == (MQKV - 1 if wbscr.name.startswith("wqkvb") else KD - 1):
            b.produced("wst_done_" + wbscr.name[:4] + "_0", (f"{pref}0", b.pc[f"{pref}0"]))
            b.produced("wst_done_" + wbscr.name[:4] + "_1", (f"{pref}1", b.pc[f"{pref}1"]))
    else:
        b.wait_free("sync", wslab)
        b.wait_ready("sync", "wst_done_" + wbscr.name[:4] + "_0")
        b.wait_ready("sync", "wst_done_" + wbscr.name[:4] + "_1")
        m_l = b.emit("sync", lambda e, d=wsb[:, sl * D:(sl + 1) * D], s=wbscr[m, :, :]:
                     e.dma_start(out=d, in_=s), inc=f"dWb{sl}", inc_k=16)
        b.produced(wslab, m_l)
    return wslab, sl


def _phase_qkv(b, ctx, chunk):
    nc = b.nc
    wqkv, wqkvb, wsb, t1f, pbig = (ctx[k] for k in ("wqkv", "wqkvb", "wsb", "t1f", "pbig"))
    dst_of = {0: ctx["qg"], 1: ctx["kg"], 2: ctx["vg"]}
    nm_of = {0: "qg", 1: "kg", 2: "vg"}
    for w in ("qg", "kg", "vg"):
        b.wait_free("scalar", w)
    last_mm = None
    for m in range(MQKV):
        wslab, sl = _stream_wslab(b, ctx, chunk, wqkv, wqkvb, m)
        which, head = divmod(m, NH)
        for sub in range(2):
            pb = (m * 2 + sub) % 3
            pbuf = ("pbig", pb)
            b.wait_free("tensor", pbuf)
            b.wait_ready("tensor", wslab)
            b.wait_ready("tensor", "t1f")
            mm = None
            for k in range(KD):
                mm = b.emit("tensor", lambda e, p=pbig[pb][:],
                            w=wsb[:, sl * D + k * 128: sl * D + (k + 1) * 128],
                            r=t1f[:, k * C + sub * 512: k * C + sub * 512 + 512],
                            st=(k == 0), sp=(k == KD - 1):
                            nc.tensor.matmul(p, w, r, start=st, stop=sp),
                            inc="sT" if k == KD - 1 else None)
            b.produced(pbuf, mm)
            last_mm = mm
            if sub == 1:
                b.consumed(wslab, mm)
            b.wait_ready("scalar", pbuf)
            dst_ap = dst_of[which][:].rearrange("p (gg t h) -> p gg t h", t=8, h=16)[
                :, 64 * sub:64 * (sub + 1), :, head:head + 1]
            scale = QSCALE if which == 0 else 1.0
            m_e = b.emit("scalar", lambda e, d=dst_ap, s=pbig[pb][:], sc=scale:
                         nc.scalar.activation(d, s, AF.Copy, scale=sc), inc="sS")
            b.consumed(pbuf, m_e)
        if chunk == 0:
            _gamma_step(b, ctx, 2)
        else:
            _quant_step(b, ctx, 2)
    for f in range(KD):
        b.consumed(("t1f", f), last_mm)
    mark = ("sS", b.pc["sS"])
    for n in ("qkvg", "qg", "kg", "vg"):
        b.produced(n, mark)


def _attn_part1(b, ctx, chunk, bt):
    nc = b.nc
    (qg, kg, pscore, sexp, sums4, wdiv, mask01) = (ctx[k] for k in (
        "qg", "kg", "pscore", "sexp", "sums4", "wdiv", "mask01"))
    if bt == 0:
        b.wait_ready("tensor", "qkvg")
        b.wait_ready("tensor", "consts")
    pb = bt % 2
    h0 = pb * 512
    ps, sx, sm, wv = ("pscore", pb), ("sexp", pb), ("sums4", pb), ("wdiv", pb)
    b.wait_free("tensor", ps)
    mm = None
    for g4 in range(4):
        gi = bt * 4 + g4
        mm = b.emit("tensor", lambda e, p=pscore[pb][:, g4 * 128:(g4 + 1) * 128],
                    q=qg[:, gi * 128:(gi + 1) * 128], k=kg[:, gi * 128:(gi + 1) * 128]:
                    nc.tensor.matmul(p, q, k, start=True, stop=True),
                    inc="sT" if g4 == 3 else None)
    b.produced(ps, mm)
    ctx["_attn_lastsc"] = mm
    b.wait_free("scalar", sx)
    b.wait_ready("scalar", ps)
    m_x = b.emit("scalar", lambda e, d=sexp[:, h0:h0 + 512], s=pscore[pb][:]:
                 nc.scalar.activation(d, s, AF.Exp), inc="sS")
    b.consumed(ps, m_x)
    b.produced(sx, m_x)
    b.wait_free("vector", wv)
    b.wait_ready("vector", sx)
    b.wait_ready("vector", "consts")
    m_mm = b.emit("vector", lambda e, d=wdiv[:, h0:h0 + 512], s=sexp[:, h0:h0 + 512],
                  mk=mask01[:]:
                  nc.vector.tensor_mul(d, s, mk), inc="sV")
    b.consumed(sx, m_mm)
    b.produced(wv, m_mm)
    b.wait_free("vector", sm)
    b.wait("vector", "sV", m_mm[1])
    m_s = b.emit("vector", lambda e, d=sums4[:, pb * 8:pb * 8 + 4],
                 s=wdiv[:, h0:h0 + 512].rearrange("p (g x) -> p g x", g=4):
                 nc.vector.tensor_reduce(d, s, AX.X, ALU.add), inc="sV")
    b.wait("vector", "sV", m_s[1])
    m_r = b.emit("vector", lambda e, d=sums4[:, pb * 8 + 4:pb * 8 + 8],
                 s=sums4[:, pb * 8:pb * 8 + 4]:
                 nc.vector.reciprocal(d, s), inc="sV")
    b.produced(sm, m_r)
    b.wait_free("vector", sx)
    m_d = None
    for g4 in range(4):
        b.wait("vector", "sV", m_r[1])
        m_d = b.emit("vector", lambda e, d=sexp[:, h0 + g4 * 128:h0 + (g4 + 1) * 128],
                     s=wdiv[:, h0 + g4 * 128:h0 + (g4 + 1) * 128],
                     r=sums4[:, pb * 8 + 4 + g4:pb * 8 + 5 + g4]:
                     nc.vector.tensor_scalar(d, s, r, None, ALU.mult), inc="sV")
    b.produced(sx, m_d)
    b.consumed(wv, m_d)
    b.consumed(sm, m_d)


def _attn_part2(b, ctx, chunk, bt):
    nc = b.nc
    (vg, pmix, sexp, wtb, vtb, ident, t1f) = (ctx[k] for k in (
        "vg", "pmix", "sexp", "wtb", "vtb", "ident", "t1f"))
    atf = t1f
    ptx = ctx["ptx"]
    pb = bt % 2
    h0 = pb * 512
    sx, wt, vt = ("sexp", pb), ("wtb",), ("vtb",)
    pv = ("ptbank", 0)
    b.wait_free("tensor", vt)
    b.wait_free("tensor", pv)
    mtv = None
    for g4 in range(4):
        gi = bt * 4 + g4
        mtv = b.emit("tensor", lambda e, p=ptx[0][:, g4 * 128:(g4 + 1) * 128],
                     s=vg[:, gi * 128:(gi + 1) * 128], idn=ident[:]:
                     nc.tensor.transpose(p, s, idn), inc="sT" if g4 == 3 else None)
    b.produced(pv, mtv)
    ctx["_attn_lasttv"] = mtv
    b.wait_ready("vector", pv)
    mev = b.emit("vector", lambda e, d=vtb[:, 0:512], s=ptx[0][:, 0:512]:
                 nc.vector.tensor_copy(d, s), inc="sV")
    b.consumed(pv, mev)
    b.produced(vt, mev)
    pw = ("ptbank", 1)
    b.wait_free("tensor", wt)
    b.wait_free("tensor", pw)
    b.wait_ready("tensor", sx)
    mtw = None
    for g4 in range(4):
        mtw = b.emit("tensor", lambda e, p=ptx[1][:, g4 * 128:(g4 + 1) * 128],
                     s=sexp[:, h0 + g4 * 128:h0 + (g4 + 1) * 128], idn=ident[:]:
                     nc.tensor.transpose(p, s, idn), inc="sT" if g4 == 3 else None)
    b.produced(pw, mtw)
    b.consumed(sx, mtw)
    b.wait_ready("vector", pw)
    mew = b.emit("vector", lambda e, d=wtb[:, 0:512], s=ptx[1][:, 0:512]:
                 nc.vector.tensor_copy(d, s), inc="sV")
    b.consumed(pw, mew)
    b.produced(wt, mew)
    b.wait_free("tensor", ("pmix",))
    b.wait_ready("tensor", wt)
    b.wait_ready("tensor", vt)
    mmx = None
    for g4 in range(4):
        mmx = b.emit("tensor", lambda e, p=pmix[:, g4 * 128:(g4 + 1) * 128],
                     v=vtb[:, g4 * 128:(g4 + 1) * 128],
                     w=wtb[:, g4 * 128:(g4 + 1) * 128]:
                     nc.tensor.matmul(p, v, w, start=True, stop=True),
                     inc="sT" if g4 == 3 else None)
    b.produced(("pmix",), mmx)
    b.consumed(wt, mmx)
    b.consumed(vt, mmx)
    b.wait_ready("scalar", ("pmix",))
    dst = atf[:].rearrange("p (h gg t) -> p gg t h", h=16, t=8)[:, bt * 4:(bt + 1) * 4, :, :]
    src = pmix[:].rearrange("p (g t h) -> p g t h", t=8, h=16)
    m_de = b.emit("scalar", lambda e, d=dst, s=src:
                  nc.scalar.activation(d, s, AF.Copy), inc="sS")
    b.consumed(("pmix",), m_de)
    if chunk == 0:
        _gamma_step(b, ctx, 3)
    else:
        _quant_step(b, ctx, 3)
    if bt == NB - 1:
        b.consumed("qg", ctx["_attn_lastsc"])
        b.consumed("kg", ctx["_attn_lastsc"])
        b.consumed("vg", ctx["_attn_lasttv"])
        b.produced("atf", m_de)
        for f in range(KD):
            b.produced(("t1f", f), m_de)


def _phase_attn_inner(b, ctx, chunk):
    _attn_part1(b, ctx, chunk, 0)
    for bt in range(1, NB):
        _attn_part1(b, ctx, chunk, bt)
        _attn_part2(b, ctx, chunk, bt - 1)
    _attn_part2(b, ctx, chunk, NB - 1)


def _phase_wout(b, ctx, chunk):
    nc = b.nc
    wout, woutb, wsb, pbig, bias16, yscr, t1f = (
        ctx[k] for k in ("wout", "woutb", "wsb", "pbig", "bias16", "yscr", "t1f"))
    atf = t1f
    ytmp = ctx["qg"]
    b.wait_free("vector", "qg")
    wr = []
    last_mm = None
    for m in range(KD):
        wslab, sl = _stream_wslab(b, ctx, chunk, wout, woutb, m)
        evs = []
        for sub in range(2):
            pb = (m * 2 + sub) % 3
            pbuf = ("pbig", pb)
            b.wait_free("tensor", pbuf)
            b.wait_ready("tensor", wslab)
            b.wait_ready("tensor", "atf")
            mm = None
            for k in range(KD):
                mm = b.emit("tensor", lambda e, p=pbig[pb][:],
                            w=wsb[:, sl * D + k * 128: sl * D + (k + 1) * 128],
                            r=atf[:, k * C + sub * 512: k * C + sub * 512 + 512],
                            st=(k == 0), sp=(k == KD - 1):
                            nc.tensor.matmul(p, w, r, start=st, stop=sp),
                            inc="sT" if k == KD - 1 else None)
            b.produced(pbuf, mm)
            last_mm = mm
            if sub == 1:
                b.consumed(wslab, mm)
            b.wait_ready("vector", pbuf)
            b.wait_ready("vector", "consts")
            m_e = b.emit("vector", lambda e, d=ytmp[:, m * C + sub * 512: m * C + sub * 512 + 512],
                         s=pbig[pb][:], bb=bias16[:, m:m + 1]:
                         nc.vector.tensor_scalar(d, s, bb, None, ALU.add), inc="sV")
            b.consumed(pbuf, m_e)
            evs.append(m_e)
        for m_e in evs:
            b.wait("gpsimd", m_e[0], m_e[1])
        b.wait_free("gpsimd", "yscr")
        m_w = b.emit("gpsimd", lambda e, d=yscr[m * 128:(m + 1) * 128, :],
                     s=ytmp[:, m * C:(m + 1) * C]: e.dma_start(out=d, in_=s),
                     inc="dYw", inc_k=16)
        wr.append(m_w)
        if chunk == 0:
            _gamma_step(b, ctx, 6)
        else:
            _quant_step(b, ctx, 2)
    b.produced("qg", ("dYw", b.pc["dYw"]))
    for f in range(KD):
        b.consumed(("t1f", f), last_mm)
    b.produced("yscr", ("dYw", b.pc["dYw"]))
    ctx["_yscr_marks"] = wr


def _phase_y_ln_resid(b, ctx, chunk):
    nc = b.nc
    yscr, xb, ttok, xbscr = (
        ctx[k] for k in ("yscr", "xb", "ttok", "xbscr"))
    act2 = ctx["act2scr"]
    if chunk == 0:
        ytok, ykn = ctx["ytok"], "xstg"
    else:
        ytok, ykn = ctx["ytokB"], "ytokB"
        b.wait_free("sync", "vg")
    t2wr = []
    b.wait_ready("sync", "yscr")
    for i in range(NTT):
        sl = i % 2
        yt = (ykn, sl)
        b.wait_free("sync", yt)
        m_tr = b.emit("sync", lambda e, d=ytok[:, sl * D:(sl + 1) * D],
                      s=yscr[:, i * 128:(i + 1) * 128]: e.dma_start_transpose(out=d, in_=s),
                      inc=f"dX{sl}", inc_k=16)
        b.produced(yt, m_tr)
        b.consumed("yscr", m_tr)
        b.wait_free("vector", ("ttok", sl))
        m_ap = _ln_stats_apply(b, ctx, ytok[:, sl * D:(sl + 1) * D], yt, i,
                               ttok[:, sl * D:(sl + 1) * D], ("ttok", sl))
        b.wait("vector", "sV", m_ap[1])
        b.wait_ready("vector", ("xb", i))
        m_x = b.emit("vector", lambda e, d=xb[:, i * D:(i + 1) * D],
                     s=xb[:, i * D:(i + 1) * D], t=ttok[:, sl * D:(sl + 1) * D]:
                     nc.vector.tensor_add(d, s, t), inc="sV")
        b.consumed(("ttok", sl), m_x)
        b.produced(("xb", i), m_x)
        b.produced(("xdata", chunk, i), m_x)
        b.wait_free("vector", ("ttok", sl))
        m_t2 = _ln_stats_apply(b, ctx, xb[:, i * D:(i + 1) * D], ("xb", i), i,
                               ttok[:, sl * D:(sl + 1) * D], ("ttok", sl))
        b.wait_ready("gpsimd", ("ttok", sl))
        m_wr = b.emit("gpsimd", lambda e,
                      d=act2[chunk * C + i * 128:chunk * C + (i + 1) * 128, :],
                      s=ttok[:, sl * D:(sl + 1) * D]: e.dma_start(out=d, in_=s),
                      inc=f"dAc2{chunk}_{sl}", inc_k=16)
        b.consumed(("ttok", sl), m_wr)
        t2wr.append(m_wr)
        b.wait("gpsimd", m_t2[0], m_t2[1])
        m_pk = b.emit("gpsimd", lambda e,
                      d=xbscr[chunk * C + i * 128:chunk * C + (i + 1) * 128, :],
                      s=xb[:, i * D:(i + 1) * D]: e.dma_start(out=d, in_=s),
                      inc=f"dPk{chunk}", inc_k=16)
        mjp = b.join("gpsimd", [m_pk])
        b.consumed(("xb", i), mjp)
        if chunk == 0:
            _gamma_step(b, ctx, 1)
    b.produced(("xpark_done", chunk), ("sJ", b.pc["sJ"]))
    ctx[f"_t2wr_{chunk}"] = t2wr
    b.produced(("act2", chunk, 0), (f"dAc2{chunk}_0", b.pc[f"dAc2{chunk}_0"]))
    b.produced(("act2", chunk, 1), (f"dAc2{chunk}_1", b.pc[f"dAc2{chunk}_1"]))


def _gamma_init(b, ctx):
    jobs = []
    for wi, wt in enumerate(("w1", "w2")):
        for s in range(KD):
            for p0 in range(0, FF, 1024):
                jobs.append((wi, wt, s, p0, min(1024, FF - p0)))
    for s in range(MFF):
        for hh in range(2):
            jobs.append((2, "w3", s, hh * 1024, 1024))
    ctx["_gjobs"] = jobs
    ctx["_gidx"] = 0
    ctx["_gcols"] = {}


def _gamma_step(b, ctx, n):
    nc = b.nc
    xstg, xcF, gpart = ctx["xstg"], ctx["xcF"], ctx["gpart"]
    ttokF = (ctx["ttf0"], ctx["ttf1"])
    slots = [(xstg[:, 0:1024], ("xstg", 0)),
             (xstg[:, 1024:2048], ("xstg", 1)),
             (xcF[:, 0:1024], ("xcst",)),
             (ttokF[0][:, 0:1024], ("ttok", 0)),
             (ttokF[1][:, 0:1024], ("ttok", 1))]
    jobs = ctx["_gjobs"]
    for _ in range(n):
        i = ctx["_gidx"]
        if i >= len(jobs):
            return
        ctx["_gidx"] = i + 1
        wi, wname, s, p0, width = jobs[i]
        wsrc = ctx[wname]
        sap, hkey = slots[i % 5]
        b.wait_free("gpsimd", hkey)
        m_l = b.emit("gpsimd", lambda e, d=sap[:, 0:width],
                     s_=wsrc[s * 128:(s + 1) * 128, p0:p0 + width]:
                     e.dma_start(out=d, in_=s_), inc=f"dG{i % 5}", inc_k=16)
        b.produced(hkey, m_l)
        b.wait_ready("scalar", hkey)
        m_a = b.emit("scalar", lambda e, d=sap[:, 0:width],
                     acc=gpart[:, i:i + 1]:
                     nc.scalar.activation(d, d, AF.Abs, accum_out=acc), inc="sS")
        b.consumed(hkey, m_a)
        b.produced(("gpart", i), m_a)
        ctx["_gcols"].setdefault(wi, []).append(i)


def _gamma_final_one(b, ctx, wi):
    nc = b.nc
    gpart, onesf, hgam, lntmp = (
        ctx[k] for k in ("gpart", "onesf", "hgam", "lntmp"))
    pmisc = ctx["pscore"][0]
    cs = ctx["_gcols"][wi]
    lo, hi = min(cs), max(cs) + 1
    assert hi - lo == len(cs) == (88 if wi == 2 else 96), (wi, len(cs))
    b.wait_ready("vector", ("gpart", hi - 1))
    b.wait_free("vector", ("lntmp", wi))
    m_r = b.emit("vector", lambda e, d=lntmp[:, wi:wi + 1], s=gpart[:, lo:hi]:
                 nc.vector.tensor_reduce(d, s, AX.X, ALU.add), inc="sV")
    b.produced(("lntmp", wi), m_r)
    b.produced(("gred", wi), m_r)
    b.wait_free("tensor", ("pscore", 0))
    b.wait_ready("tensor", ("gred", wi))
    b.wait_ready("tensor", "ones")
    m_mm = b.emit("tensor", lambda e, p=pmisc[0:1, wi:wi + 1], o=onesf[:, 0:1],
                  s=lntmp[:, wi:wi + 1]:
                  nc.tensor.matmul(p, o, s, start=True, stop=True), inc="sT")
    b.produced(("gsum", wi), m_mm)
    b.consumed(("lntmp", wi), m_mm)
    b.wait_ready("vector", ("gsum", wi))
    b.wait_free("vector", ("lntmp", 4 + wi))
    m_h = b.emit("vector", lambda e, d=lntmp[0:1, 4 + wi:5 + wi], s=pmisc[0:1, wi:wi + 1]:
                 nc.vector.tensor_scalar(d, s, 0.5 / NW1, 0.5 * EPS_Q, ALU.mult, ALU.add),
                 inc="sV")
    b.consumed(("gsum", wi), m_h)
    b.produced(("hrow", wi), m_h)
    b.produced(("lntmp", 4 + wi), m_h)
    b.wait_ready("tensor", ("hrow", wi))
    m_b = b.emit("tensor", lambda e, p=pmisc[:, 4 + wi:5 + wi], o=onesf[0:1, :],
                 s=lntmp[0:1, 4 + wi:5 + wi]:
                 nc.tensor.matmul(p, o, s, start=True, stop=True), inc="sT")
    b.produced(("hbc", wi), m_b)
    b.consumed(("lntmp", 4 + wi), m_b)
    b.wait_ready("vector", ("hbc", wi))
    m_c = b.emit("vector", lambda e, d=hgam[:, wi:wi + 1], s=pmisc[:, 4 + wi:5 + wi]:
                 nc.vector.tensor_copy(d, s), inc="sV")
    b.consumed(("hbc", wi), m_c)
    m_n2 = b.emit("vector", lambda e, d=hgam[:, 4 + wi:5 + wi], s=pmisc[:, 4 + wi:5 + wi]:
                  nc.vector.tensor_scalar(d, s, -1.0, None, ALU.mult), inc="sV")
    b.consumed(("hbc", wi), m_n2)
    b.consumed(("pscore", 0), m_n2)
    b.produced(("hgam", wi), ("sV", b.pc["sV"]))


def _quant_init(b, ctx):
    jobs = []
    for g in range(3):
        gend = min(2048 * (g + 1), FF)
        for wi, wn, qn in ((0, "w1", "qw1T"), (1, "w2", "qw2T")):
            for si in range(KD):
                for p0 in range(2048 * g, gend, 1024):
                    w = min(1024, gend - p0)
                    last = (si == KD - 1 and p0 + w == gend)
                    jobs.append((wi, ctx[wn], ctx[qn], si, p0, w,
                                 (wi, g) if last else None))
    for si in range(MFF):
        for hh in range(2):
            last = (si == MFF - 1 and hh == 1)
            jobs.append((2, ctx["w3"], ctx["qw3T"], si, hh * 1024, 1024,
                         (2, 0) if last else None))
    ctx["_qjobs"] = jobs
    ctx["_qidx"] = 0
    ctx["_qin"] = 0


def _quant_step(b, ctx, n):
    nc = b.nc
    jobs = ctx.get("_qjobs")
    if not jobs:
        return
    wsf, xstg, xcst, hgam, ttok = (
        ctx[k] for k in ("wsf", "xstg", "xcst", "hgam", "ttok"))
    slots = [(wsf[:, 0:1024], ("wsf", 0)),
             (wsf[:, 1024:2048], ("wsf", 1)),
             (xstg[:, 0:1024], ("xstg", 0)),
             (xstg[:, 1024:2048], ("xstg", 1))]
    N = len(jobs)

    def emit_in(j):
        wi_, wsrc_, _qd, si_, p0_, width_, _g = jobs[j]
        sap_, fsl_ = slots[j % 4]
        b.wait_free("gpsimd", fsl_)
        m = b.emit("gpsimd", lambda e, d=sap_[:, 0:width_],
                   s_=wsrc_[si_ * 128:(si_ + 1) * 128, p0_:p0_ + width_]:
                   e.dma_start(out=d, in_=s_), inc=f"dQf{j % 4}", inc_k=16)
        b.produced(fsl_, m)

    def pump_in(limit):
        while ctx["_qin"] < min(limit, N):
            emit_in(ctx["_qin"])
            ctx["_qin"] += 1

    for _ in range(n):
        idx = ctx["_qidx"]
        if idx >= N:
            return
        pump_in(idx + 4)
        ctx["_qidx"] = idx + 1
        wi, wsrc, qdstT, si, p0, width, grp = jobs[idx]
        sap, fsl = slots[idx % 4]
        b.wait_ready("vector", fsl)
        b.wait_ready("vector", ("hgam", wi))
        b.wait_free("vector", ("xcst",))
        b.wait_free("vector", ("xq", 0))
        m_s = b.emit("vector", lambda e, d=xcst[:, 0:width],
                     s_=sap[:, 0:width], hg=hgam[:, wi:wi + 1]:
                     nc.vector.tensor_scalar(d, s_, hg, None, ALU.is_gt), inc="sV")
        b.produced(("xq", 0), m_s)
        b.wait_free("vector", ("xq", 1))
        m_g = b.emit("vector", lambda e, d=xcst[:, 1024:1024 + width],
                     s_=sap[:, 0:width], nhg=hgam[:, 4 + wi:5 + wi]:
                     nc.vector.tensor_scalar(d, s_, nhg, None, ALU.is_lt), inc="sV")
        b.produced(("xq", 1), m_g)
        b.consumed(fsl, m_g)
        osl = ("ttok", idx % 2)
        b.wait_free("vector", osl)
        b.wait_ready("vector", ("xq", 0))
        b.wait_ready("vector", ("xq", 1))
        m_q = b.emit("vector", lambda e, d=ttok[:, (idx % 2) * D:(idx % 2) * D + width],
                     a=xcst[:, 0:width], gg=xcst[:, 1024:1024 + width]:
                     nc.vector.tensor_sub(d, a, gg), inc="sV")
        b.consumed(("xq", 0), m_q)
        b.consumed(("xq", 1), m_q)
        b.produced(osl, m_q)
        b.wait_ready("gpsimd", osl)
        m_w = b.emit("gpsimd", lambda e, d=qdstT[si * 128:(si + 1) * 128, p0:p0 + width],
                     s_=ttok[:, (idx % 2) * D:(idx % 2) * D + width]:
                     e.dma_start(out=d, in_=s_), inc=f"dQs{idx % 2}", inc_k=16)
        b.consumed(osl, m_w)
        if grp is not None:
            mj = b.join("gpsimd", [(f"dQs{k}", b.pc.get(f"dQs{k}", 0))
                                   for k in range(2)])
            if grp == (2, 0):
                b.produced("qwT3_done_0", mj)
                b.produced("qwT3_done_1", mj)
            else:
                b.produced(("qwTr", grp[0] + 1, grp[1]), mj)


def _ffn_t2f_load(b, ctx, chunk):
    t2f = ctx["t1f"]
    act2 = ctx["act2scr"]
    b.wait_ready("sync", ("act2", chunk, 0))
    b.wait_ready("sync", ("act2", chunk, 1))
    for f in range(KD):
        b.wait_free("sync", ("t1f", f))
        b.emit("sync", lambda e, d=t2f[:, f * C:(f + 1) * C],
               s=act2[chunk * C:(chunk + 1) * C, f * 128:(f + 1) * 128]:
               e.dma_start_transpose(out=d, in_=s),
               inc="dTf", inc_k=16)
    setm = ("dTf", b.pc["dTf"])
    for f in range(KD):
        b.produced(("t1f", f), setm)
    b.produced("t2f", setm)


def _phase_ffn(b, ctx, chunk):
    nc = b.nc
    (t1f, hid, sigst, qw1, qw2, qw3,
     yscr, ytok, ttok, xb, xbscr, ofin, y_out) = (ctx[k] for k in (
        "t1f", "hid", "sigst", "qw1", "qw2", "qw3",
        "yscr", "ytok", "ttok", "xb", "xbscr", "ofin", "y_out"))
    band3 = ctx["band3"]
    bands = [[ctx["bandw1a"], ctx["bandw1b"]], [ctx["bandw2a"], ctx["bandw2b"]]]
    wT = [ctx["qw1T"], ctx["qw2T"]]
    pbig = ctx["pbig4"] + [ctx["pmix"], ctx["pscore"][1]]
    t2f = t1f
    if chunk == 0:
        b.wait_free("tensor", ("pscore", 0))
        b.wait_free("tensor", ("pscore", 1))
        b.wait_free("tensor", ("pmix",))
        for k in ("sexp", "wdiv", "wtb", "vtb"):
            b.wait_free("scalar", (k,))
        for k in ("qg", "kg", "vg"):
            for eng in ("vector", "scalar", "sync", "gpsimd"):
                b.wait_free(eng, k)
        for eng in ("sync", "scalar"):
            b.wait_free(eng, ("wsb", 0))
            b.wait_free(eng, ("wsb", 1))
    if not ctx.get(f"_t2f_loaded_{chunk}"):
        _ffn_t2f_load(b, ctx, chunk)

    NMC = FF // 256
    out_wr = []
    last_w1w2_mm = None
    for mc in range(NMC):
        for wi in range(2):
            ring = "sync"
            if chunk == 0:
                dsem = f"dBS{wi}{mc % 2}"
                bkey = ("bandS", wi, mc % 2)
                bandbuf = ((ctx["bandw1b"], ctx["bandw1a"]),
                           (ctx["bandw2a"], ctx["bandw2b"]))[wi][mc % 2]
                if mc % 2 == 1:
                    if wi == 0:
                        b.wait_free(ring, ("wsf", 0))
                        b.wait_free(ring, ("wsf", 1))
                    else:
                        b.wait_free(ring, ("xstg", 0))
                        b.wait_free(ring, ("xstg", 1))
            else:
                dsem = f"dB{wi + 1}{mc % 2}"
                bkey = ("band", wi, mc % 2)
                bandbuf = bands[wi][mc % 2]
            b.wait_free(ring, bkey)
            if wi == 0 and mc % 2 == 0:
                b.wait_free(ring, ("band3", 0))
            elif wi == 0:
                b.wait_free(ring, ("band3", 0))
                b.wait_free(ring, ("band3", 1))
            elif mc % 2 == 1:
                b.wait_free(ring, ("band3", 1))
            b.wait_free(ring, ("ofin", 0))
            b.wait_free(ring, ("ofin", 1))
            if chunk == 1:
                b.wait_free(ring, ("wsf", 0))
                b.wait_free(ring, ("wsf", 1))
                b.wait_free(ring, ("xstg", 0))
                b.wait_free(ring, ("xstg", 1))
                b.wait_free(ring, ("bandS", wi, 0))
                b.wait_free(ring, ("bandS", wi, 1))
            if chunk == 0 and wi == 0:
                b.wait_free(ring, ("wsb", 0))
                b.wait_free(ring, ("wsb", 1))
            b.wait_ready(ring, ("qwTr", wi + 1, min(2, (mc * 256) // 2048)))
            src_r = wT[wi].rearrange("(k p) m -> p k m", p=128)
            mlast = None
            for g2 in range(2):
                mlast = b.emit(ring, lambda e,
                               d=bandbuf[:].rearrange("p (k m) -> p k m", m=256)[:, g2 * 8:(g2 + 1) * 8, :],
                               s=src_r[:, g2 * 8:(g2 + 1) * 8, mc * 256:(mc + 1) * 256]:
                               e.dma_start(out=d, in_=s), inc=dsem, inc_k=16)
            b.produced(bkey, mlast)
            for mt in range(2):
                for sub in range(2):
                    pb = (mt * 2 + sub) % 3 + 3 * wi
                    pbuf = ("pbig", pb)
                    b.wait_free("tensor", pbuf)
                    b.wait_ready("tensor", bkey)
                    b.wait_ready("tensor", "t2f")
                    mm = None
                    for k in range(KD):
                        mm = b.emit("tensor", lambda e, p=pbig[pb][:],
                                    w=bandbuf[:, k * 256 + mt * 128: k * 256 + (mt + 1) * 128],
                                    r=t2f[:, k * C + sub * 512: k * C + sub * 512 + 512],
                                    st=(k == 0), sp=(k == KD - 1):
                                    nc.tensor.matmul(p, w, r, start=st, stop=sp),
                                    inc="sT" if k == KD - 1 else None)
                    b.produced(pbuf, mm)
                    last_w1w2_mm = mm
                    if mt == 1 and sub == 1:
                        b.consumed(bkey, mm)
                    scol = (mt * 2 + sub) * 512
                    if wi == 0:
                        b.wait_ready("scalar", pbuf)
                        b.wait_free("scalar", ("sigst", mt * 2 + sub))
                        m_sg = b.emit("scalar", lambda e, d=sigst[:, scol:scol + 512],
                                      s=pbig[pb][:]: nc.scalar.activation(d, s, AF.Sigmoid),
                                      inc="sS")
                        b.consumed(pbuf, m_sg)
                        b.produced(("sigst", mt * 2 + sub), m_sg)
                    else:
                        b.wait_ready("vector", pbuf)
                        b.wait_ready("vector", ("sigst", mt * 2 + sub))
                        hcol = (mc * 2 + mt) * C + sub * 512
                        b.wait_free("vector", ("hid", mc * 2 + mt))
                        if mc * 2 + mt == 32:
                            b.wait_free("vector", ("ytokB", 0))
                            b.wait_free("vector", ("ytokB", 1))
                        if mc * 2 + mt == 36:
                            b.wait_free("vector", ("ofinC0", 0))
                            b.wait_free("vector", ("ofinC0", 1))
                        m_h = b.emit("vector", lambda e, d=hid[:, hcol:hcol + 512],
                                     a=sigst[:, scol:scol + 512], p=pbig[pb][:]:
                                     nc.vector.tensor_mul(d, a, p), inc="sV")
                        b.consumed(pbuf, m_h)
                        b.consumed(("sigst", mt * 2 + sub), m_h)
                        if sub == 1:
                            b.produced(("hid", mc * 2 + mt), m_h)
                        else:
                            b.produced(("hidA", mc * 2 + mt), m_h)
    for f in range(KD):
        b.consumed(("t1f", f), last_w1w2_mm)
    NDC = D // 128
    src3 = ctx["qw3T"].rearrange("(k p) m -> p k m", p=128)
    for dc in range(NDC):
        b3 = ("band3", dc % 2)
        ring, dsem = (("sync", "dB3a") if dc % 2 == 0 else ("sync", "dB3b"))
        b.wait_free(ring, b3)
        b.wait_free(ring, ("band", 0, 0))
        b.wait_free(ring, ("band", 0, 1))
        b.wait_free(ring, ("band", 1, 0))
        b.wait_free(ring, ("band", 1, 1))
        b.wait_free(ring, ("bandS", 0, 0))
        b.wait_free(ring, ("bandS", 0, 1))
        b.wait_free(ring, ("bandS", 1, 1))
        b.wait_free(ring, ("ofin", 0))
        b.wait_free(ring, ("ofin", 1))
        b.wait_free(ring, ("xstg", 0))
        b.wait_free(ring, ("xstg", 1))
        b.wait_ready(ring, "qwT3_done_0")
        b.wait_ready(ring, "qwT3_done_1")
        base3 = (dc % 2) * MFF * 128
        mlast = None
        for g4 in range(4):
            mlast = b.emit(ring, lambda e,
                           d=band3[:, base3:base3 + MFF * 128].rearrange("p (k m) -> p k m", m=128)[:, g4 * 11:(g4 + 1) * 11, :],
                           s=src3[:, g4 * 11:(g4 + 1) * 11, dc * 128:(dc + 1) * 128]:
                           e.dma_start(out=d, in_=s), inc=dsem, inc_k=16)
        b.produced(b3, mlast)
        if chunk == 0 and dc == 1:
            _ffn_t2f_load(b, ctx, 1)
            ctx["_t2f_loaded_1"] = True
        for sub in range(2):
            mg = dc * 2 + sub
            pb = mg % 4
            pbuf = ("pbig", pb)
            b.wait_free("tensor", pbuf)
            b.wait_ready("tensor", b3)
            for k in range(MFF):
                b.wait_ready("tensor", ("hid", k))
            mm = None
            for k in range(MFF):
                mm = b.emit("tensor", lambda e, p=pbig[pb][:],
                            w=band3[:, base3 + k * 128:base3 + (k + 1) * 128],
                            r=hid[:, k * C + sub * 512: k * C + sub * 512 + 512],
                            st=(k == 0), sp=(k == MFF - 1):
                            nc.tensor.matmul(p, w, r, start=st, stop=sp),
                            inc="sT" if k == MFF - 1 else None)
            b.produced(pbuf, mm)
            if sub == 1:
                b.consumed(b3, mm)
            if dc == NDC - 1 and sub == 1:
                for k in range(MFF):
                    b.consumed(("hid", k), mm)
            b.wait_ready("vector", pbuf)
            osl = ("ttok", mg % 2)
            b.wait_free("vector", osl)
            m_e = b.emit("vector", lambda e, d=ttok[:, (mg % 2) * D:(mg % 2) * D + 512],
                         s=pbig[pb][:]: nc.vector.tensor_copy(d, s), inc="sV")
            b.consumed(pbuf, m_e)
            b.produced(osl, m_e)
            b.wait_ready("gpsimd", osl)
            b.wait_free("gpsimd", "yscr")
            m_w = b.emit("gpsimd", lambda e,
                         d=yscr[dc * 128:(dc + 1) * 128, sub * 512:sub * 512 + 512],
                         s=ttok[:, (mg % 2) * D:(mg % 2) * D + 512]:
                         e.dma_start(out=d, in_=s), inc="dYw", inc_k=16)
            mj = b.join("gpsimd", [m_w])
            b.consumed(osl, mj)
            out_wr.append(m_w)
    b.produced("yscr", ("dYw", b.pc["dYw"]))
    if chunk == 0:
        ytl, ytkn = ctx["ytokB"], "ytokB"
        ofb, ofkn = ctx["ofinC0"], "ofinC0"
    else:
        ytl, ytkn = ctx["ytok"], "xstg"
        ofb, ofkn = ctx["ofin"], "ofin"
    for i in range(NTT):
        b.wait_ready("scalar", ("xpark_done", chunk))
        b.wait_free("scalar", ("xb", i))
        b.emit("scalar", lambda e, d=xb[:, i * D:(i + 1) * D],
               s=xbscr[chunk * C + i * 128:chunk * C + (i + 1) * 128, :]:
               e.dma_start(out=d, in_=s), inc=f"dRs{chunk}", inc_k=16)
    m_xr = (f"dRs{chunk}", b.pc[f"dRs{chunk}"])
    for i in range(NTT):
        b.produced(("xb", i), m_xr)
    b.wait_ready("sync", "yscr")
    for i in range(NTT):
        sl = i % 2
        yt = (ytkn, sl)
        b.wait_free("sync", yt)
        if chunk == 1:
            b.wait_free("sync", ("band3", 0))
            b.wait_free("sync", ("band3", 1))
            b.wait_free("sync", ("band", 1, 1))
        m_tr = b.emit("sync", lambda e, d=ytl[:, sl * D:(sl + 1) * D],
                      s=yscr[:, i * 128:(i + 1) * 128]: e.dma_start_transpose(out=d, in_=s),
                      inc=f"dX{sl}", inc_k=16)
        b.produced(yt, m_tr)
        b.consumed("yscr", m_tr)
        b.wait_free("vector", ("ttok", sl))
        m_ap = _ln_stats_apply(b, ctx, ytl[:, sl * D:(sl + 1) * D], yt, i,
                               ttok[:, sl * D:(sl + 1) * D], ("ttok", sl))
        b.wait("vector", "sV", m_ap[1])
        b.wait_ready("vector", ("xb", i))
        osl = (ofkn, sl)
        b.wait_free("vector", osl)
        if chunk == 1:
            b.wait_free("vector", ("band3", 0))
            b.wait_free("vector", ("band3", 1))
            b.wait_free("vector", ("band", 0, 0))
            b.wait_free("vector", ("band", 0, 1))
            b.wait_free("vector", ("wsf", 0))
            b.wait_free("vector", ("wsf", 1))
            b.wait_free("vector", ("wsb", 0))
            b.wait_free("vector", ("wsb", 1))
        m_o = b.emit("vector", lambda e, d=ofb[:, sl * D:(sl + 1) * D],
                     s=xb[:, i * D:(i + 1) * D], t=ttok[:, sl * D:(sl + 1) * D]:
                     nc.vector.tensor_add(d, s, t), inc="sV")
        b.consumed(("ttok", sl), m_o)
        b.consumed(("xb", i), m_o)
        b.produced(osl, m_o)
        b.wait_ready("gpsimd", osl)
        m_out = b.emit("gpsimd", lambda e,
                       d=y_out[chunk * C + i * 128:chunk * C + (i + 1) * 128, :],
                       s=ofb[:, sl * D:(sl + 1) * D]: e.dma_start(out=d, in_=s),
                       inc="dO", inc_k=16)
        mj = b.join("gpsimd", [m_out])
        b.consumed(osl, mj)


_NC_CACHE = {}


def _get_program(dbg=None):
    if dbg not in _NC_CACHE:
        _NC_CACHE[dbg] = build_program(dbg)
    return _NC_CACHE[dbg]


def _make_consts():
    ident = np.eye(128, dtype=bfdt)
    mask = np.zeros((128, 512), dtype=bfdt)
    for gg in range(4):
        for t in range(8):
            mask[t * 16:(t + 1) * 16, gg * 128 + t * 16:gg * 128 + (t + 1) * 16] = 1
    return ident, mask


def kernel(L0_attn_x, L0_attn_W_qkv, L0_attn_W_out, L0_attn_b_out,
           L0_ffn_w1, L0_ffn_w2, L0_ffn_w3):
    x = np.ascontiguousarray(np.asarray(L0_attn_x, dtype=np.float32).reshape(16384, 2048))
    ident, mask = _make_consts()
    bout_t = np.ascontiguousarray(np.asarray(L0_attn_b_out, dtype=np.float32).reshape(16, 128).T)
    w1t = np.ascontiguousarray(np.asarray(L0_ffn_w1, dtype=np.float32).T)
    w2t = np.ascontiguousarray(np.asarray(L0_ffn_w2, dtype=np.float32).T)
    w3t = np.ascontiguousarray(np.asarray(L0_ffn_w3, dtype=np.float32).T)
    nc, b = _get_program(DEBUG_STAGE)
    in_maps = []
    for c in range(NCORES):
        in_maps.append(dict(
            x=x[c * T:(c + 1) * T],
            wqkv=np.asarray(L0_attn_W_qkv, dtype=np.float32),
            wout=np.asarray(L0_attn_W_out, dtype=np.float32),
            bout=bout_t,
            w1=w1t, w2=w2t, w3=w3t,
            ident=ident, mask01=mask,
        ))
    res = run_bass_kernel_spmd(nc, in_maps, core_ids=list(range(NCORES)))
    out = np.concatenate([r["y"] for r in res.results], axis=0)
    return out.reshape(4, 4096, 2048)
